# version 01
# baseline (speedup 1.0000x reference)
import math
from contextlib import ExitStack

import numpy as np
import ml_dtypes
import concourse.bass as bass
import concourse.mybir as mybir
from concourse.bass_utils import run_bass_kernel_spmd

F32 = mybir.dt.float32
BF16 = mybir.dt.bfloat16
AF = mybir.ActivationFunctionType
ALU = mybir.AluOpType
AX = mybir.AxisListType
BF = ml_dtypes.bfloat16

B, L, D = 4, 4096, 1024
DEBUG = False
S7A_TRUNC = 5
S3_TRUNC = 0
FCONST_VALUES = (1e-5, -0.5, 0.5, 1e-6, 0.125, 0.015625, 0.03125)
TLOC = L // 2
DCH = D // 128
NCORES = 8

ENGINES = ("sync", "tensor", "vector", "scalar", "gpsimd")


class Buf:
    __slots__ = ("name", "last_w", "readers")

    def __init__(self, name=""):
        self.name = name
        self.last_w = None
        self.readers = []


NGPS = 8
SAME_DIST = 4


class Prog:
    def __init__(self, nc):
        self.nc = nc
        self.steps = {e: [] for e in ENGINES}
        self.ccount = {e: 0 for e in ENGINES}
        self.dcount = {e: 0 for e in ENGINES}
        self.gcount = {e: [0] * NGPS for e in ENGINES}
        self.waited = {e: {} for e in ENGINES}

    def _collect(self, e, reads, writes, is_dma=False):
        deps = {}
        for b in reads:
            if b.last_w is not None:
                deps[b.last_w] = True
        for b in writes:
            if b.last_w is not None:
                deps[b.last_w] = True
            for r in b.readers:
                deps[r] = True
        waits = []
        for (kind, de, v) in deps:
            if de == e and kind == "c":
                if e == "tensor" or self.ccount[e] + 1 - v > SAME_DIST:
                    continue
            tgt = v if kind == "c" else 16 * v
            key = (kind, de)
            if self.waited[e].get(key, 0) < tgt:
                waits.append((key, tgt))
                self.waited[e][key] = tgt
        return waits


    def _finish(self, tok, reads, writes):
        for b in reads:
            b.readers.append(tok)
        for b in writes:
            b.last_w = tok
            b.readers = []
        return tok

    def op(self, e, emit, reads=(), writes=(), tag=""):
        waits = self._collect(e, reads, writes)
        self.ccount[e] += 1
        tok = ("c", e, self.ccount[e])
        self.steps[e].append((waits, emit, "c", tag))
        return self._finish(tok, reads, writes)

    def dma(self, e, out, in_, reads=(), writes=(), tag="", accum=False):
        waits = self._collect(e, reads, writes, is_dma=True)
        s = sum(self.gcount[e]) % NGPS
        prev = self.gcount[e][s]
        self.gcount[e][s] += 1
        tok = ("g", (e, s), self.gcount[e][s])
        key = ("g", (e, s))
        if prev > 0 and self.waited[e].get(key, 0) < 16 * prev:
            waits = list(waits) + [(key, 16 * prev)]

        def emit(eng, out=out, in_=in_, accum=accum):
            if accum:
                return eng.dma_start(out=out, in_=in_, accum_op=ALU.add)
            return eng.dma_start(out=out, in_=in_)

        self.steps[e].append((waits, emit, ("g", (e, s)), tag))
        return self._finish(tok, reads, writes)

    def raw(self, e, emit, reads=(), writes=(), tag=""):
        waits = self._collect(e, reads, writes)
        if waits:
            self.steps[e].append((waits, lambda eng: None, "none", tag))

    def barrier(self):
        snap = [(("c", e), self.ccount[e]) for e in ENGINES] + \
               [(("g", (e, s)), 16 * self.gcount[e][s])
                for e in ENGINES for s in range(NGPS)]
        for e in ENGINES:
            waits = []
            for key, tgt in snap:
                if key == ("c", e):
                    continue
                if tgt > 0 and self.waited[e].get(key, 0) < tgt:
                    waits.append((key, tgt))
                    self.waited[e][key] = tgt
            if waits:
                self.steps[e].append((waits, lambda eng: None, "none", "barrier"))

    def finish_outputs(self, tokens=()):
        waits = []
        for e in ENGINES:
            for s in range(NGPS):
                if self.gcount[e][s] == 0:
                    continue
                key = ("g", (e, s))
                tgt = 16 * self.gcount[e][s]
                if self.waited["sync"].get(key, 0) < tgt:
                    self.waited["sync"][key] = tgt
                    waits.append((key, tgt))
        if waits:
            self.steps["sync"].append((tuple(waits), lambda eng: None,
                                       "none", "fin"))

    def emit(self, es):
        nc = self.nc
        sems = {}
        for e in ENGINES:
            sems[("c", e)] = es.enter_context(nc.semaphore(f"c_{e}"))
        for e in ENGINES:
            for s in range(NGPS):
                if self.gcount[e][s] > 0:
                    sems[("g", (e, s))] = es.enter_context(
                        nc.semaphore(f"g_{e}_{s}"))
        block = es.enter_context(nc.Block())

        def mk_section(e):
            def section(eng):
                for waits, emit, kind, tag in self.steps[e]:
                    for key, tgt in waits:
                        eng.wait_ge(sems[key], tgt)
                    inst = emit(eng)
                    if kind == "c":
                        if inst is None:
                            raise RuntimeError(f"step '{tag}' on {e}: no inst")
                        inst.then_inc(sems[("c", e)], 1)
                    elif kind == "d":
                        inst.then_inc(sems[("d", e)], 16)
                    elif isinstance(kind, tuple) and kind[0] == "g":
                        inst.then_inc(sems[("g", kind[1])], 16)
            return section

        for e in ENGINES:
            if self.steps[e]:
                getattr(block, e)(mk_section(e))


class Ctx:
    def __init__(self, nc, es, p):
        self.nc, self.es, self.p = nc, es, p

    def sbuf(self, name, shape, dtype, es=None):
        t = (es or self.es).enter_context(self.nc.sbuf_tensor(name, shape, dtype))
        return t, Buf(name)

    def psum(self, name, shape, dtype=F32, es=None):
        t = (es or self.es).enter_context(self.nc.psum_tensor(name, shape, dtype))
        return t, Buf(name)



def _ct(a):
    return np.ascontiguousarray(a)


def blk128(v):
    k = v.shape[0] // 128
    return _ct(v.reshape(k, 128).T)


CONST_SHAPES = {
    "ones_col_bf": ([128, 1], BF16), "ones_col_f": ([128, 1], F32),
    "ones_row_f": ([1, 128], F32), "ones_row_bf": ([1, 128], BF16),
    "ones_row512": ([1, 512], BF16),
    "ident_f": ([128, 128], F32), "ident_bf": ([128, 128], BF16),
}


def host_consts():
    return {
        "ones_col_bf": np.ones((128, 1), dtype=BF),
        "ones_col_f": np.ones((128, 1), dtype=np.float32),
        "ones_row_f": np.ones((1, 128), dtype=np.float32),
        "ones_row_bf": np.ones((1, 128), dtype=BF),
        "ones_row512": np.ones((1, 512), dtype=BF),
        "ident_f": np.eye(128, dtype=np.float32),
        "ident_bf": np.eye(128, dtype=BF),
    }


def load_consts(ctx, tags):
    out = {}
    for t in tags:
        sh, dt = CONST_SHAPES[t]
        ap = ctx.nc.dram_tensor(t, sh, dt, kind="ExternalInput").ap()
        tens, buf = ctx.sbuf("c_" + t, sh, dt)
        ctx.p.dma("sync", out=tens[:], in_=ap, writes=[buf], tag="ld_" + t)
        out[t] = (tens, buf)
    return out



def load_fconsts(ctx, values):
    nc, p = ctx.nc, ctx.p
    k = len(values)
    ap = nc.dram_tensor("fconsts", [128, k], F32, kind="ExternalInput").ap()
    fcs, fcs_b = ctx.sbuf("fconsts_sb", [128, k], F32)
    p.dma("sync", out=fcs[:], in_=ap, writes=[fcs_b], tag="ld_fconsts")
    for i, v in enumerate(values):
        nc.const_aps.register(F32, float(v), fcs[:, i:i + 1])
    for e in ("tensor", "vector", "scalar", "gpsimd"):
        ctx.p.raw(e, lambda eng: None, reads=[fcs_b], tag="fc_wait")
    return values


def host_fconsts(values):
    a = np.zeros((128, len(values)), np.float32)
    for i, v in enumerate(values):
        a[:, i] = v
    return a



def make_row_scratch(ctx, width=1024):
    nmf, nmf_b = ctx.sbuf("rs_nmf", [1, width], F32)
    m2, m2_b = ctx.sbuf("rs_m2", [1, width], F32)
    var, var_b = ctx.sbuf("rs_var", [1, width], F32)
    return dict(nmf=(nmf, nmf_b), m2=(m2, m2_b), var=(var, var_b))


def emit_row_stats(ctx, pref, T, nfeat, rs, ps_s, ps_s_b, ps_q, ps_q_b,
                   negmean, negmean_b, rstdf, rstdf_b, std=None, std_b=None,
                   eps=1e-5):
    p = ctx.p
    nmf, nmf_b = rs["nmf"]
    m2, m2_b = rs["m2"]
    var, var_b = rs["var"]
    inv = 1.0 / nfeat
    p.op("vector", lambda v: v.tensor_scalar_mul(nmf[:, 0:T],
                                                 ps_s[0:1, 0:T], -inv),
         reads=[ps_s_b], writes=[nmf_b], tag=pref + "nm")
    p.op("vector", lambda v: v.tensor_copy(negmean, nmf[:, 0:T]),
         reads=[nmf_b], writes=[negmean_b], tag=pref + "nmbf")
    p.op("scalar", lambda s: s.activation(out=m2[:, 0:T], in_=nmf[:, 0:T],
                                          func=AF.Square),
         reads=[nmf_b], writes=[m2_b], tag=pref + "m2")
    p.op("vector", lambda v: v.tensor_scalar_mul(var[:, 0:T],
                                                 ps_q[0:1, 0:T], inv),
         reads=[ps_q_b], writes=[var_b], tag=pref + "ex2")
    p.op("vector", lambda v: v.tensor_sub(var[:, 0:T], var[:, 0:T], m2[:, 0:T]),
         reads=[var_b, m2_b], writes=[var_b], tag=pref + "var")
    p.op("scalar", lambda s: s.activation(out=m2[:, 0:T], in_=var[:, 0:T],
                                          func=AF.Ln, bias=float(eps)),
         reads=[var_b], writes=[m2_b], tag=pref + "lnv")
    p.op("scalar", lambda s: s.activation(out=rstdf, in_=m2[:, 0:T],
                                          func=AF.Exp, scale=-0.5),
         reads=[m2_b], writes=[rstdf_b], tag=pref + "rstd")
    if std is not None:
        p.op("scalar", lambda s: s.activation(out=std, in_=m2[:, 0:T],
                                              func=AF.Exp, scale=0.5),
             reads=[m2_b], writes=[std_b], tag=pref + "std")



def build_bitnet():
    nc = bass.Bass("TRN2", target_bir_lowering=False, debug=False)
    T = TLOC
    TH = T // 2
    H = 4096
    HCH = H // 128

    xT = nc.dram_tensor("xT", [D, T], F32, kind="ExternalInput").ap()
    wfcT = nc.dram_tensor("wfcT", [D, H], F32, kind="ExternalInput").ap()
    wprT = nc.dram_tensor("wprT", [H, D], F32, kind="ExternalInput").ap()
    lnw_b = nc.dram_tensor("lnw_b", [128, HCH], F32, kind="ExternalInput").ap()
    lnb_b = nc.dram_tensor("lnb_b", [128, HCH], F32, kind="ExternalInput").ap()
    yT = nc.dram_tensor("yT", [D, T], F32, kind="ExternalOutput").ap()
    hs = nc.dram_tensor("hs", [H, T], BF16)
    w2s = nc.dram_tensor("w2s", [H, D], BF16)
    if DEBUG:
        dthr = nc.dram_tensor("dthr", [128, 4], F32, kind="ExternalOutput").ap()
        drows = nc.dram_tensor("drows", [8, T], F32, kind="ExternalOutput").ap()
        dwq = nc.dram_tensor("dwq", [128, H], F32, kind="ExternalOutput").ap()
        dcs = nc.dram_tensor("dcs", [4, H], F32, kind="ExternalOutput").ap()
        dh = nc.dram_tensor("dh", [H, T], F32, kind="ExternalOutput").ap()
        dw2 = nc.dram_tensor("dw2", [H, D], F32, kind="ExternalOutput").ap()

    es = ExitStack()
    es_ps = ExitStack()
    p = Prog(nc)
    ctx = Ctx(nc, es, p)
    consts = load_consts(ctx, ["ones_col_bf", "ones_col_f", "ones_row_f"])
    load_fconsts(ctx, FCONST_VALUES)
    ones_bf, ones_bf_bb = consts["ones_col_bf"]
    ones_f, ones_f_bb = consts["ones_col_f"]
    ones_row, ones_row_bb = consts["ones_row_f"]

    x_bf, _ = ctx.sbuf("x_bf", [128, DCH * T], BF16)
    xbufs = [Buf(f"x_bf{c}") for c in range(DCH)]
    x2, x2_b = ctx.sbuf("x2", [128, TH], BF16)
    rstd_bc, rstd_bc_b = ctx.sbuf("rstd_bc", [128, T], F32)
    rstd2_bc, rstd2_bc_b = ctx.sbuf("rstd2_bc", [128, T], F32)
    lnw_sb, lnw_sb_b = ctx.sbuf("lnw_sb", [128, HCH], F32)
    lnb_f, lnb_f_b = ctx.sbuf("lnb_f", [128, HCH], BF16)
    absacc, absacc_b = ctx.sbuf("absacc", [128, 16], F32)
    gacc, gacc_b = ctx.sbuf("gacc", [128, 8], F32)
    abscol, abscol_b = ctx.sbuf("abscol", [128, 2], F32)
    gtmp, gtmp_b = ctx.sbuf("gtmp", [1, 4], F32)
    thr_fc, thr_fc_b = ctx.sbuf("thr_fc", [128, 2], F32)
    thr_pr, thr_pr_b = ctx.sbuf("thr_pr", [128, 2], F32)
    colsum1, colsum1_b = ctx.sbuf("colsum1", [1, H], BF16)
    colsum2, colsum2_b = ctx.sbuf("colsum2", [1, D], BF16)
    colvec3, colvec3_b = ctx.sbuf("colvec3", [1, D], BF16)
    negmean, negmean_b = ctx.sbuf("negmean", [1, T], BF16)
    rstdf, rstdf_b = ctx.sbuf("rstdf", [1, T], F32)
    negmean2, negmean2_b = ctx.sbuf("negmean2", [1, T], BF16)
    std2, std2_b = ctx.sbuf("std2", [1, T], BF16)
    rs = make_row_scratch(ctx)
    hf32, hf32_b = ctx.sbuf("hf32", [128, TH], F32)
    hbf, _hb = ctx.sbuf("hbf", [128, 2 * TH], BF16)
    hbf_bufs = [Buf("hbf0"), Buf("hbf1")]
    xres, xres_b = ctx.sbuf("xres", [128, 512], F32)
    of32, of32_b = ctx.sbuf("of32", [128, 512], F32)
    ysb, _ys = ctx.sbuf("ysb", [128, 2 * 512], F32)
    ysb_bufs = [Buf("ysb0"), Buf("ysb1")]
    wtmp, wtmp_b = ctx.sbuf("wtmp", [128, 2048], F32)
    qt1, qt1_b = ctx.sbuf("qt1", [128, 2048], BF16)
    qt2, qt2_b = ctx.sbuf("qt2", [128, 2048], BF16)
    w2g, _w2 = ctx.sbuf("w2g", [128, 2 * D], BF16)
    w2g_bufs = [Buf("w2g0"), Buf("w2g1")]
    es_w = ExitStack()
    wq_fc, _ = ctx.sbuf("wq_fc", [128, DCH * H], BF16, es=es_w)
    wqfc_bufs = [Buf(f"wqfc{c}") for c in range(DCH)]

    ps = [ctx.psum(f"ps{i}", [128, 512], F32) for i in range(4)]
    psR1, psR1_b = ctx.psum("psR1", [1, TH], F32, es=es_ps)
    psR2, psR2_b = ctx.psum("psR2", [1, TH], F32, es=es_ps)

    for c in range(DCH):
        p.dma("gpsimd", out=x_bf[:, c * T:(c + 1) * T],
              in_=xT[c * 128:(c + 1) * 128, :], writes=[xbufs[c]], tag=f"ldx{c}")
    p.dma("sync", out=lnw_sb[:], in_=lnw_b, writes=[lnw_sb_b], tag="ldlnw")
    p.dma("gpsimd", out=lnb_f[:], in_=lnb_b, writes=[lnb_f_b], tag="ldlnbf")

    def gamma_pass(tag, src_ap, nrowchunk, W, kcol):
        units = [(c, u0, min(2048, W - u0))
                 for c in range(nrowchunk) for u0 in range(0, W, 2048)]
        g = 0
        p.op("vector", lambda v: v.memset(gacc[:], 0.0), writes=[gacc_b],
             tag=f"{tag}gz")
        for gi in range(0, len(units), 16):
            grp = units[gi:gi + 16]
            for i, (c, u0, w) in enumerate(grp):
                p.dma("sync", out=wtmp[:, 0:w],
                      in_=src_ap[c * 128:(c + 1) * 128, u0:u0 + w],
                      writes=[wtmp_b], tag=f"{tag}ld{gi + i}")

                def em(s, w=w, i=i):
                    return s.activation(out=qt1[:, 0:w], in_=wtmp[:, 0:w],
                                        func=AF.Abs,
                                        accum_out=absacc[:, i:i + 1])
                p.op("scalar", em, reads=[wtmp_b], writes=[absacc_b, qt1_b],
                     tag=f"{tag}abs{gi + i}")
            n = len(grp)
            if n < 16:
                p.op("vector", lambda v, n=n: v.memset(absacc[:, n:16], 0.0),
                     writes=[absacc_b], tag=f"{tag}az{g}")

            def emr(v, g=g):
                return v.reduce_sum(gacc[:, g:g + 1], absacc[:], axis=AX.X)
            p.op("vector", emr, reads=[absacc_b], writes=[gacc_b],
                 tag=f"{tag}red{g}")
            g += 1

        def emf(v, kcol=kcol):
            return v.reduce_sum(abscol[:, kcol:kcol + 1], gacc[:], axis=AX.X)
        p.op("vector", emf, reads=[gacc_b], writes=[abscol_b], tag=f"{tag}fin")

    gamma_pass("gfc", wfcT, DCH, H, 0)
    gamma_pass("gpr", wprT, HCH, D, 1)

    def em_tot(te):
        te.matmul(psR1[:, 0:1], abscol[:, 0:1], ones_f[:], start=True, stop=True)
        return te.matmul(psR1[:, 1:2], abscol[:, 1:2], ones_f[:], start=True,
                         stop=True)
    p.op("tensor", em_tot, reads=[abscol_b, ones_f_bb], writes=[psR1_b],
         tag="gtot")
    NW = float(D * H)

    def em_thr(v):
        return v.tensor_scalar(gtmp[:, 0:2], psR1[0:1, 0:2], 0.5 / NW, 0.5e-6,
                               ALU.mult, ALU.add)
    p.op("vector", em_thr, reads=[psR1_b], writes=[gtmp_b], tag="gthr")

    def em_thrb(te):
        te.matmul(ps[0][0][:, 0:1], ones_row[:], gtmp[:, 0:1], start=True,
                  stop=True)
        return te.matmul(ps[0][0][:, 1:2], ones_row[:], gtmp[:, 1:2],
                         start=True, stop=True)
    p.op("tensor", em_thrb, reads=[gtmp_b, ones_row_bb], writes=[ps[0][1]],
         tag="gthrb")
    for (thr, thr_b, col) in [(thr_fc, thr_fc_b, 0), (thr_pr, thr_pr_b, 1)]:
        p.op("vector", lambda v, thr=thr, col=col: v.tensor_copy(
            thr[:, 0:1], ps[0][0][:, col:col + 1]),
            reads=[ps[0][1]], writes=[thr_b], tag=f"thrc{col}")
        p.op("vector", lambda v, thr=thr: v.tensor_scalar_mul(
            thr[:, 1:2], thr[:, 0:1], -1.0),
            reads=[thr_b], writes=[thr_b], tag=f"thrn{col}")

    for hh in range(2):
        off = hh * TH
        for c in range(DCH):
            xs = x_bf[:, c * T + off: c * T + off + TH]

            def em_sum(te, xs=xs, c=c):
                last = None
                for j in range(0, TH, 512):
                    last = te.matmul(psR1[:, j:j + 512], ones_bf[:],
                                     xs[:, j:j + 512], start=(c == 0),
                                     stop=(c == DCH - 1))
                return last
            p.op("tensor", em_sum, reads=[xbufs[c], ones_bf_bb],
                 writes=[psR1_b], tag=f"osum{hh}_{c}")

            def em_sq(s, xs=xs):
                return s.activation(out=x2[:], in_=xs, func=AF.Square)
            p.op("scalar", em_sq, reads=[xbufs[c]], writes=[x2_b],
                 tag=f"osq{hh}_{c}")

            def em_sumq(te, c=c):
                last = None
                for j in range(0, TH, 512):
                    last = te.matmul(psR2[:, j:j + 512], ones_bf[:],
                                     x2[:, j:j + 512], start=(c == 0),
                                     stop=(c == DCH - 1))
                return last
            p.op("tensor", em_sumq, reads=[x2_b, ones_bf_bb], writes=[psR2_b],
                 tag=f"osumq{hh}_{c}")
        emit_row_stats(ctx, f"ost{hh}", TH, D, rs, psR1, psR1_b, psR2, psR2_b,
                       negmean[:, off:off + TH], negmean_b,
                       rstdf[:, off:off + TH], rstdf_b)
    for j in range(0, T, 512):
        def em(te, j=j):
            return te.matmul(ps[1][0][:], ones_row[:], rstdf[:, j:j + 512],
                             start=True, stop=True)
        p.op("tensor", em, reads=[rstdf_b, ones_row_bb], writes=[ps[1][1]],
             tag=f"rbc{j}")
        p.op("vector", lambda v, j=j: v.tensor_copy(rstd_bc[:, j:j + 512],
                                                    ps[1][0][:]),
             reads=[ps[1][1]], writes=[rstd_bc_b], tag=f"rbe{j}")

    def quant_unit(tag, src_slice, src_buf, dst_slice, dst_buf, w, thr, thr_b):
        p.op("vector", lambda v: v.tensor_single_scalar(
            qt1[:, 0:w], src_slice, thr[:, 0:1], ALU.is_gt),
            reads=[src_buf, thr_b], writes=[qt1_b], tag=tag + "gt")
        p.op("vector", lambda v: v.tensor_single_scalar(
            qt2[:, 0:w], src_slice, thr[:, 1:2], ALU.is_lt),
            reads=[src_buf, thr_b], writes=[qt2_b], tag=tag + "lt")
        p.op("vector", lambda v: v.tensor_sub(dst_slice, qt1[:, 0:w],
                                              qt2[:, 0:w]),
             reads=[qt1_b, qt2_b], writes=[dst_buf], tag=tag + "sub")

    ui = 0
    for c in range(DCH):
        for u0 in range(0, H, 2048):
            p.dma("sync", out=wtmp[:, 0:2048],
                  in_=wfcT[c * 128:(c + 1) * 128, u0:u0 + 2048],
                  writes=[wtmp_b], tag=f"qfcld{ui}")
            quant_unit(f"qfc{ui}", wtmp[:, 0:2048], wtmp_b,
                       wq_fc[:, c * H + u0: c * H + u0 + 2048],
                       wqfc_bufs[c], 2048, thr_fc, thr_fc_b)
            ui += 1
    for r in range(4):
        for c in range(DCH):
            def em(te, r=r, c=c):
                last = None
                for j in range(2):
                    last = te.matmul(psR1[:, j * 512:(j + 1) * 512], ones_bf[:],
                                     wq_fc[:, c * H + r * 1024 + j * 512:
                                           c * H + r * 1024 + (j + 1) * 512],
                                     start=(c == 0), stop=(c == DCH - 1))
                return last
            p.op("tensor", em, reads=[wqfc_bufs[c], ones_bf_bb],
                 writes=[psR1_b], tag=f"cs1_{r}_{c}")
        p.op("vector", lambda v, r=r: v.tensor_copy(
            colsum1[:, r * 1024:(r + 1) * 1024], psR1[:]),
            reads=[psR1_b], writes=[colsum1_b], tag=f"cs1e{r}")

    for c in range(HCH):
        bi = c % 2
        sl = w2g[:, bi * D:(bi + 1) * D]
        p.dma("sync", out=wtmp[:, 0:D],
              in_=wprT[c * 128:(c + 1) * 128, :], writes=[wtmp_b],
              tag=f"qprld{c}")
        quant_unit(f"qpr{c}", wtmp[:, 0:D], wtmp_b, sl, w2g_bufs[bi], D,
                   thr_pr, thr_pr_b)

        def em_cv(te, c=c, sl=sl):
            last = None
            for j in range(2):
                last = te.matmul(psR2[:, j * 512:(j + 1) * 512],
                                 lnb_f[:, c:c + 1],
                                 sl[:, j * 512:(j + 1) * 512],
                                 start=(c == 0), stop=(c == HCH - 1))
            return last
        p.op("tensor", em_cv, reads=[w2g_bufs[bi], lnb_f_b], writes=[psR2_b],
             tag=f"cv3_{c}")

        def em_lnw(v, c=c, sl=sl):
            return v.tensor_single_scalar(sl, sl, lnw_sb[:, c:c + 1], ALU.mult)
        p.op("vector", em_lnw, reads=[w2g_bufs[bi], lnw_sb_b],
             writes=[w2g_bufs[bi]], tag=f"lnw{c}")

        def em_cs2(te, c=c, sl=sl):
            last = None
            for j in range(2):
                last = te.matmul(psR1[:, j * 512:(j + 1) * 512], ones_bf[:],
                                 sl[:, j * 512:(j + 1) * 512],
                                 start=(c == 0), stop=(c == HCH - 1))
            return last
        p.op("tensor", em_cs2, reads=[w2g_bufs[bi], ones_f_bb],
             writes=[psR1_b], tag=f"cs2_{c}")
        p.dma("sync", out=w2s[c * 128:(c + 1) * 128, :], in_=sl,
              reads=[w2g_bufs[bi]], tag=f"w2st{c}")
    p.op("vector", lambda v: v.tensor_copy(colsum2[:], psR1[:]),
         reads=[psR1_b], writes=[colsum2_b], tag="cs2e")
    p.op("vector", lambda v: v.tensor_copy(colvec3[:], psR2[:]),
         reads=[psR2_b], writes=[colvec3_b], tag="cv3e")

    allw = list(wqfc_bufs)
    for hh in range(2):
        off = hh * TH
        for hc in range(HCH):
            pp = (ps[0], ps[1]) if hc % 2 == 0 else (ps[2], ps[3])

            def em_h(te, hc=hc, off=off, pp=pp):
                for c in range(DCH):
                    for j in range(2):
                        te.matmul(pp[j][0][:],
                                  wq_fc[:, c * H + hc * 128:
                                        c * H + hc * 128 + 128],
                                  x_bf[:, c * T + off + j * 512:
                                       c * T + off + (j + 1) * 512],
                                  start=(c == 0), stop=False)
                last = None
                for j in range(2):
                    last = te.matmul(pp[j][0][:],
                                     colsum1[:, hc * 128: hc * 128 + 128],
                                     negmean[:, off + j * 512:
                                             off + (j + 1) * 512],
                                     start=False, stop=True)
                return last
            p.op("tensor", em_h, reads=xbufs + allw + [colsum1_b, negmean_b],
                 writes=[pp[0][1], pp[1][1]], tag=f"h{hh}_{hc}")

            def em_ev(v, pp=pp, off=off):
                v.tensor_mul(hf32[:, 0:512], pp[0][0][:],
                             rstd_bc[:, off:off + 512])
                return v.tensor_mul(hf32[:, 512:1024], pp[1][0][:],
                                    rstd_bc[:, off + 512:off + 1024])
            p.op("vector", em_ev, reads=[pp[0][1], pp[1][1], rstd_bc_b],
                 writes=[hf32_b], tag=f"hev{hh}_{hc}")
            hbi = hc % 2

            def em_g(s, hbi=hbi):
                return s.activation(out=hbf[:, hbi * TH:(hbi + 1) * TH],
                                    in_=hf32[:], func=AF.Gelu)
            p.op("scalar", em_g, reads=[hf32_b], writes=[hbf_bufs[hbi]],
                 tag=f"hg{hh}_{hc}")

            def em_hsum(te, hbi=hbi, hc=hc):
                last = None
                for j in range(2):
                    last = te.matmul(psR1[:, j * 512:(j + 1) * 512], ones_bf[:],
                                     hbf[:, hbi * TH + j * 512:
                                          hbi * TH + (j + 1) * 512],
                                     start=(hc == 0), stop=(hc == HCH - 1))
                return last
            p.op("tensor", em_hsum, reads=[hbf_bufs[hbi], ones_bf_bb],
                 writes=[psR1_b], tag=f"hsum{hh}_{hc}")

            def em_hsq(s, hbi=hbi):
                return s.activation(out=x2[:],
                                    in_=hbf[:, hbi * TH:(hbi + 1) * TH],
                                    func=AF.Square)
            p.op("scalar", em_hsq, reads=[hbf_bufs[hbi]], writes=[x2_b],
                 tag=f"hsq{hh}_{hc}")

            def em_hsumq(te, hc=hc):
                last = None
                for j in range(2):
                    last = te.matmul(psR2[:, j * 512:(j + 1) * 512], ones_bf[:],
                                     x2[:, j * 512:(j + 1) * 512],
                                     start=(hc == 0), stop=(hc == HCH - 1))
                return last
            p.op("tensor", em_hsumq, reads=[x2_b, ones_bf_bb], writes=[psR2_b],
                 tag=f"hsumq{hh}_{hc}")
            p.dma("gpsimd", out=hs[hc * 128:(hc + 1) * 128, off:off + TH],
                  in_=hbf[:, hbi * TH:(hbi + 1) * TH], reads=[hbf_bufs[hbi]],
                  tag=f"hst{hh}_{hc}")
        emit_row_stats(ctx, f"hstat{hh}", TH, H, rs, psR1, psR1_b, psR2,
                       psR2_b, negmean2[:, off:off + TH], negmean2_b,
                       rstdf[:, off:off + TH], rstdf_b,
                       std=std2[:, off:off + TH], std_b=std2_b)
        for j in range(0, TH, 512):
            def em(te, j=j, off=off):
                return te.matmul(ps[1][0][:], ones_row[:],
                                 rstdf[:, off + j:off + j + 512],
                                 start=True, stop=True)
            p.op("tensor", em, reads=[rstdf_b, ones_row_bb], writes=[ps[1][1]],
                 tag=f"r2bc{hh}_{j}")
            p.op("vector", lambda v, j=j, off=off: v.tensor_copy(
                rstd2_bc[:, off + j:off + j + 512], ps[1][0][:]),
                reads=[ps[1][1]], writes=[rstd2_bc_b], tag=f"r2be{hh}_{j}")

    p.barrier()
    es_ps.close()
    es_w.close()
    ps2 = [ctx.psum(f"psx{i}", [128, 512], F32) for i in range(4)]
    psO = ps + ps2
    hq, _hq = ctx.sbuf("hq", [128, HCH * 512], BF16)
    hq_bufs = [Buf(f"hq_{c}") for c in range(HCH)]

    out_toks = []
    NQ = T // 512
    for q in range(NQ):
        qoff = q * 512
        for hc in range(HCH):
            p.dma("sync", out=hq[:, hc * 512:(hc + 1) * 512],
                  in_=hs[hc * 128:(hc + 1) * 128, qoff:qoff + 512],
                  writes=[hq_bufs[hc]], tag=f"hld{q}_{hc}")
        for hc in range(HCH):
            bi = hc % 2
            p.dma("gpsimd", out=w2g[:, bi * D:(bi + 1) * D],
                  in_=w2s[hc * 128:(hc + 1) * 128, :], writes=[w2g_bufs[bi]],
                  tag=f"w2ld{q}_{hc}")

            def em_o(te, hc=hc, qoff=qoff, bi=bi):
                last = None
                for dc in range(DCH):
                    last = te.matmul(psO[dc][0][:],
                                     w2g[:, bi * D + dc * 128:
                                          bi * D + dc * 128 + 128],
                                     hq[:, hc * 512:(hc + 1) * 512],
                                     start=(hc == 0), stop=False)
                if hc == HCH - 1:
                    for dc in range(DCH):
                        te.matmul(psO[dc][0][:],
                                  colsum2[:, dc * 128:(dc + 1) * 128],
                                  negmean2[:, qoff:qoff + 512],
                                  start=False, stop=False)
                        last = te.matmul(psO[dc][0][:],
                                         colvec3[:, dc * 128:(dc + 1) * 128],
                                         std2[:, qoff:qoff + 512],
                                         start=False, stop=True)
                return last
            p.op("tensor", em_o,
                 reads=[hq_bufs[hc], w2g_bufs[bi], colsum2_b, colvec3_b,
                        negmean2_b, std2_b],
                 writes=[b for _, b in psO], tag=f"o{q}_{hc}")
        for dc in range(DCH):
            p.dma("sync", out=xres[:],
                  in_=xT[dc * 128:(dc + 1) * 128, qoff:qoff + 512],
                  writes=[xres_b], tag=f"xld{q}_{dc}")

            def em_m(v, dc=dc, qoff=qoff):
                return v.tensor_mul(of32[:], psO[dc][0][:],
                                    rstd2_bc[:, qoff:qoff + 512])
            p.op("vector", em_m, reads=[psO[dc][1], rstd2_bc_b],
                 writes=[of32_b], tag=f"om{q}_{dc}")
            yi = dc % 2

            def em_a(v, yi=yi):
                return v.tensor_add(ysb[:, yi * 512:(yi + 1) * 512], of32[:],
                                    xres[:])
            p.op("vector", em_a, reads=[of32_b, xres_b], writes=[ysb_bufs[yi]],
                 tag=f"oa{q}_{dc}")
            tok = p.dma("sync", out=yT[dc * 128:(dc + 1) * 128, qoff:qoff + 512],
                        in_=ysb[:, yi * 512:(yi + 1) * 512],
                        reads=[ysb_bufs[yi]], tag=f"yst{q}_{dc}")
            out_toks.append(tok)

    if DEBUG:
        dbg, dbg_b = ctx.sbuf("dbg", [128, 4096], F32)
        def dump_row(tag, row_t, row_b, width, dst):
            p.op("vector", lambda v: v.tensor_copy(dbg[0:1, 0:width], row_t),
                 reads=[row_b], writes=[dbg_b], tag="dbg" + tag)
            p.dma("sync", out=dst, in_=dbg[0:1, 0:width], reads=[dbg_b],
                  tag="dbgd" + tag)
        p.barrier()
        p.op("vector", lambda v: v.tensor_copy(dbg[:, 0:1], thr_fc[:, 0:1]),
             reads=[thr_fc_b], writes=[dbg_b], tag="dbg_t1")
        p.op("vector", lambda v: v.tensor_copy(dbg[:, 1:2], thr_pr[:, 0:1]),
             reads=[thr_pr_b], writes=[dbg_b], tag="dbg_t2")
        p.op("vector", lambda v: v.tensor_copy(dbg[:, 2:3], abscol[:, 0:1]),
             reads=[abscol_b], writes=[dbg_b], tag="dbg_t3")
        p.op("vector", lambda v: v.tensor_copy(dbg[:, 3:4], abscol[:, 1:2]),
             reads=[abscol_b], writes=[dbg_b], tag="dbg_t4")
        p.dma("sync", out=dthr, in_=dbg[:, 0:4], reads=[dbg_b], tag="dbgthr")
        dump_row("nm", negmean[:], negmean_b, T, drows[0:1, :])
        dump_row("rs", rstd_bc[0:1, 0:T], rstd_bc_b, T, drows[1:2, :])
        dump_row("nm2", negmean2[:], negmean2_b, T, drows[2:3, :])
        dump_row("sd2", std2[:], std2_b, T, drows[3:4, :])
        dump_row("rs2", rstd2_bc[0:1, 0:T], rstd2_bc_b, T, drows[4:5, :])
        p.op("vector", lambda v: v.tensor_copy(dbg[:, 0:H], wq_fc[:, 0:H]),
             reads=[wqfc_bufs[0]], writes=[dbg_b], tag="dbg_wq")
        p.dma("sync", out=dwq, in_=dbg[:, 0:H], reads=[dbg_b], tag="dbgwq")
        dump_row("cs1", colsum1[:], colsum1_b, H, dcs[0:1, :])
        dump_row("cs2", colsum2[:], colsum2_b, D, dcs[1:2, 0:D])
        dump_row("cv3", colvec3[:], colvec3_b, D, dcs[2:3, 0:D])
        for hc in range(HCH):
            p.dma("gpsimd", out=dbg[:, 0:T], in_=hs[hc * 128:(hc + 1) * 128, :],
                  writes=[dbg_b], tag=f"dbg_hld{hc}")
            p.dma("sync", out=dh[hc * 128:(hc + 1) * 128, :], in_=dbg[:, 0:T],
                  reads=[dbg_b], tag=f"dbg_hst{hc}")
            p.dma("gpsimd", out=dbg[:, 0:D], in_=w2s[hc * 128:(hc + 1) * 128, :],
                  writes=[dbg_b], tag=f"dbg_wld{hc}")
            p.dma("sync", out=dw2[hc * 128:(hc + 1) * 128, :], in_=dbg[:, 0:D],
                  reads=[dbg_b], tag=f"dbg_wst{hc}")
    p.finish_outputs(out_toks[-1:])
    p.emit(es)
    es.close()
    return nc


def host_bitnet_inmaps(X, inputs, lname):
    consts = host_consts()
    shared = dict(wfcT=_ct(inputs[f"{lname}_W_fc"].T.astype(np.float32)),
                  wprT=_ct(inputs[f"{lname}_W_proj"].T.astype(np.float32)),
                  lnw_b=blk128(np.asarray(inputs[f"{lname}_ln_w"], np.float32)),
                  lnb_b=blk128(np.asarray(inputs[f"{lname}_ln_b"], np.float32)),
                  fconsts=host_fconsts(FCONST_VALUES),
                  **{k: consts[k] for k in ["ones_col_bf", "ones_col_f",
                                            "ones_row_f"]})
    return [dict(xT=X[c], **shared) for c in range(NCORES)]



def build_wcast(nbytes_shapes):
    nc = bass.Bass("TRN2", target_bir_lowering=False, debug=False)
    es = ExitStack()
    p = Prog(nc)
    ctx = Ctx(nc, es, p)
    buf, _ = ctx.sbuf("wc", [128, 2 * 8192], BF16)
    bufs = [Buf("wc0"), Buf("wc1")]
    toks = []
    for name, (rows, cols) in nbytes_shapes:
        fin = nc.dram_tensor(name, [rows, cols], F32, kind="ExternalInput").ap()
        fout = nc.dram_tensor(name + "_bf", [rows, cols], BF16,
                              kind="ExternalOutput").ap()
        i = 0
        for r0 in range(0, rows, 128):
            for c0 in range(0, cols, 8192):
                w = min(8192, cols - c0)
                bi = i % 2
                p.dma("gpsimd", out=buf[:, bi * 8192: bi * 8192 + w],
                      in_=fin[r0:r0 + 128, c0:c0 + w], writes=[bufs[bi]],
                      tag=f"ci{name}{i}")
                toks.append(p.dma("sync", out=fout[r0:r0 + 128, c0:c0 + w],
                                  in_=buf[:, bi * 8192: bi * 8192 + w],
                                  reads=[bufs[bi]], tag=f"co{name}{i}"))
                i += 1
    p.finish_outputs(toks[-1:])
    p.emit(es)
    es.close()
    return nc


def run_wcast(inputs):
    w1 = np.asarray(inputs["L1_ffn_w1"], np.float32)
    w2 = np.asarray(inputs["L1_ffn_w2"], np.float32)
    w1t = _ct(w1.reshape(8, 8, 128, 32, 128).transpose(0, 3, 2, 1, 4))
    w2t = _ct(w2.reshape(8, 32, 128, 8, 128).transpose(0, 3, 2, 1, 4))
    shapes = [("w1t", [32 * 128, 8 * 128]), ("w2t", [8 * 128, 32 * 128])]
    key = "wcast"
    if key not in _CACHE:
        _CACHE[key] = build_wcast(shapes)
    im = [dict(w1t=w1t[c].reshape(32 * 128, 8 * 128),
               w2t=w2t[c].reshape(8 * 128, 32 * 128)) for c in range(NCORES)]
    res = run_bass_kernel_spmd(_CACHE[key], im, core_ids=list(range(NCORES)))
    w1bf = np.stack([res.results[c]["w1t_bf"].reshape(32, 128, 8, 128)
                     for c in range(NCORES)])
    w2bf = np.stack([res.results[c]["w2t_bf"].reshape(8, 128, 32, 128)
                     for c in range(NCORES)])
    return {"w1bf": w1bf, "w2bf": w2bf}



def build_moe():
    nc = bass.Bass("TRN2", target_bir_lowering=False, debug=False)
    T = TLOC
    TH = T // 2
    H = 4096
    HCH = 32
    E = 8
    BIG = 65536.0

    xT = nc.dram_tensor("xT", [D, T], F32, kind="ExternalInput").ap()
    gwT = nc.dram_tensor("gwT", [D, E], F32, kind="ExternalInput").ap()
    w1bf = nc.dram_tensor("w1bf", [E * HCH * 128, 8 * 128], BF16,
                          kind="ExternalInput").ap()
    w2bf = nc.dram_tensor("w2bf", [E * 8 * 128, HCH * 128], BF16,
                          kind="ExternalInput").ap()
    b1r = nc.dram_tensor("b1r", [E, H], F32, kind="ExternalInput").ap()
    b2r = nc.dram_tensor("b2r", [E, D], F32, kind="ExternalInput").ap()
    sel8d = nc.dram_tensor("sel8", [8, 1024], BF16, kind="ExternalInput").ap()
    yT = nc.dram_tensor("yT", [D, T], F32, kind="ExternalOutput").ap()
    yacc = nc.dram_tensor("yacc", [D, T], F32)

    es = ExitStack()
    es_st = ExitStack()
    p = Prog(nc)
    ctx = Ctx(nc, es, p)
    consts = load_consts(ctx, ["ones_col_bf", "ones_row_f", "ones_row_bf",
                               "ident_f"])
    load_fconsts(ctx, FCONST_VALUES)
    ones_bf, ones_bf_bb = consts["ones_col_bf"]
    ones_row, ones_row_bb = consts["ones_row_f"]
    ones_row_b16, ones_row_b16_bb = consts["ones_row_bf"]
    ident_f, ident_f_bb = consts["ident_f"]

    x_bf, _ = ctx.sbuf("x_bf", [128, DCH * T], BF16)
    xbufs = [Buf(f"x_bf{c}") for c in range(DCH)]
    h_sb, _ = ctx.sbuf("h_sb", [128, HCH * TH], BF16)
    hbufs = [Buf(f"h{c}") for c in range(HCH)]
    w1blk, _ = ctx.sbuf("w1blk", [128, 2 * 1024], BF16)
    w1bufs = [Buf("w1b0"), Buf("w1b1")]
    w2col, _ = ctx.sbuf("w2col", [128, 2 * 4096], BF16)
    w2bufs = [Buf("w2c0"), Buf("w2c1")]
    negmean, negmean_b = ctx.sbuf("negmean", [1, T], BF16)
    rstdf, rstdf_b = ctx.sbuf("rstdf", [1, T], F32)
    stdr, stdr_b = ctx.sbuf("stdr", [1, T], BF16)
    nm_bc, nm_bc_b = ctx.sbuf("nm_bc", [128, T], BF16)
    rstd_cols, rstd_cols_b = ctx.sbuf("rstd_cols", [128, 16], F32)
    gw_sb, gw_sb_b = ctx.sbuf("gw_sb", [128, DCH * E], BF16)
    gwork, gwork_b = ctx.sbuf("gwork", [128, 8], F32)
    gp, gp_b = ctx.sbuf("gp", [128, 8], F32)
    gtmp8, gtmp8_b = ctx.sbuf("gtmp8", [128, 8], F32)
    gcol, gcol_b = ctx.sbuf("gcol", [128, 3], F32)
    m8, m8_b = ctx.sbuf("m8", [8, T], BF16)
    mr8, mr8_b = ctx.sbuf("mr8", [8, T], BF16)
    mrbc, mrbc_b = ctx.sbuf("mrbc", [128, T], BF16)
    sel8, sel8_b = ctx.sbuf("sel8s", [8, 1024], BF16)
    b1row, b1row_b = ctx.sbuf("b1row", [1, H], BF16)
    b2sb, b2sb_b = ctx.sbuf("b2sb", [8, D], BF16)
    osb, _osb = ctx.sbuf("osb", [128, 2 * 512], F32)
    osb_bufs = [Buf("osb0"), Buf("osb1")]
    xres, xres_b = ctx.sbuf("xres", [128, 512], F32)
    yin, yin_b = ctx.sbuf("yin", [128, 512], F32)
    ysb, _ys = ctx.sbuf("ysb", [128, 2 * 512], F32)
    ysb_bufs = [Buf("ysb0"), Buf("ysb1")]
    rs = make_row_scratch(ctx)
    x2, x2_b = ctx.sbuf("x2", [128, TH], BF16, es=es_st)

    ps = [ctx.psum(f"ps{i}", [128, 512], F32) for i in range(4)]
    psR1, psR1_b = ctx.psum("psR1", [1, TH], F32, es=es_st)
    psR2, psR2_b = ctx.psum("psR2", [1, TH], F32, es=es_st)

    for c in range(DCH):
        p.dma("gpsimd", out=x_bf[:, c * T:(c + 1) * T],
              in_=xT[c * 128:(c + 1) * 128, :], writes=[xbufs[c]], tag=f"ldx{c}")
    for c in range(DCH):
        p.dma("gpsimd", out=gw_sb[:, c * E:(c + 1) * E],
              in_=gwT[c * 128:(c + 1) * 128, :], writes=[gw_sb_b], tag=f"ldgw{c}")
    p.dma("sync", out=sel8[:], in_=sel8d, writes=[sel8_b], tag="ldsel")
    p.dma("gpsimd", out=b2sb[:], in_=b2r, writes=[b2sb_b], tag="ldb2")

    for hh in range(2):
        off = hh * TH
        for c in range(DCH):
            xs = x_bf[:, c * T + off: c * T + off + TH]

            def em_sum(te, xs=xs, c=c):
                last = None
                for j in range(0, TH, 512):
                    last = te.matmul(psR1[:, j:j + 512], ones_bf[:],
                                     xs[:, j:j + 512], start=(c == 0),
                                     stop=(c == DCH - 1))
                return last
            p.op("tensor", em_sum, reads=[xbufs[c], ones_bf_bb],
                 writes=[psR1_b], tag=f"osum{hh}_{c}")

            def em_sq(s, xs=xs):
                return s.activation(out=x2[:], in_=xs, func=AF.Square)
            p.op("scalar", em_sq, reads=[xbufs[c]], writes=[x2_b],
                 tag=f"osq{hh}_{c}")

            def em_sumq(te, c=c):
                last = None
                for j in range(0, TH, 512):
                    last = te.matmul(psR2[:, j:j + 512], ones_bf[:],
                                     x2[:, j:j + 512], start=(c == 0),
                                     stop=(c == DCH - 1))
                return last
            p.op("tensor", em_sumq, reads=[x2_b, ones_bf_bb], writes=[psR2_b],
                 tag=f"osumq{hh}_{c}")
        emit_row_stats(ctx, f"ost{hh}", TH, D, rs, psR1, psR1_b, psR2, psR2_b,
                       negmean[:, off:off + TH], negmean_b,
                       rstdf[:, off:off + TH], rstdf_b,
                       std=stdr[:, off:off + TH], std_b=stdr_b)

    def em_rt(te):
        last = None
        for t in range(16):
            last = te.transpose(ps[0][0][:, t:t + 1],
                                rstdf[0:1, t * 128:(t + 1) * 128],
                                ident_f[0:1, 0:1])
        return last
    p.op("tensor", em_rt, reads=[rstdf_b, ident_f_bb], writes=[ps[0][1]],
         tag="rt")
    p.op("vector", lambda v: v.tensor_copy(rstd_cols[:], ps[0][0][:, 0:16]),
         reads=[ps[0][1]], writes=[rstd_cols_b], tag="rte")

    for j in range(0, T, 512):
        def em(te, j=j):
            return te.matmul(ps[1][0][:], ones_row_b16[:],
                             negmean[0:1, j:j + 512], start=True, stop=True)
        p.op("tensor", em, reads=[negmean_b, ones_row_b16_bb],
             writes=[ps[1][1]], tag=f"nbc{j}")
        p.op("vector", lambda v, j=j: v.tensor_copy(nm_bc[:, j:j + 512],
                                                    ps[1][0][:]),
             reads=[ps[1][1]], writes=[nm_bc_b], tag=f"nbe{j}")
    for c in range(DCH):
        def em(v, c=c):
            return v.tensor_add(x_bf[:, c * T:(c + 1) * T],
                                x_bf[:, c * T:(c + 1) * T], nm_bc[:])
        p.op("vector", em, reads=[xbufs[c], nm_bc_b], writes=[xbufs[c]],
             tag=f"xc{c}")

    for t in range(16):
        def em_g(te, t=t):
            last = None
            for c in range(DCH):
                last = te.matmul(ps[2][0][:, 0:E],
                                 x_bf[:, c * T + t * 128: c * T + (t + 1) * 128],
                                 gw_sb[:, c * E:(c + 1) * E],
                                 start=(c == 0), stop=(c == DCH - 1))
            return last
        p.op("tensor", em_g, reads=xbufs + [gw_sb_b], writes=[ps[2][1]],
             tag=f"g{t}")
        p.op("vector", lambda v, t=t: v.tensor_single_scalar(
            gwork[:], ps[2][0][:, 0:E], rstd_cols[:, t:t + 1], ALU.mult),
            reads=[ps[2][1], rstd_cols_b], writes=[gwork_b], tag=f"gl{t}")
        p.op("scalar", lambda s: s.activation(out=gwork[:], in_=gwork[:],
                                              func=AF.Exp),
             reads=[gwork_b], writes=[gwork_b], tag=f"ge{t}")
        p.op("vector", lambda v: v.reduce_sum(gcol[:, 0:1], gwork[:], axis=AX.X),
             reads=[gwork_b], writes=[gcol_b], tag=f"gs{t}")
        p.op("vector", lambda v: v.reciprocal(gcol[:, 1:2], gcol[:, 0:1]),
             reads=[gcol_b], writes=[gcol_b], tag=f"gr{t}")
        p.op("vector", lambda v: v.tensor_single_scalar(
            gp[:], gwork[:], gcol[:, 1:2], ALU.mult),
            reads=[gwork_b, gcol_b], writes=[gp_b], tag=f"gp{t}")
        p.op("vector", lambda v: v.reduce_max(gcol[:, 2:3], gp[:], axis=AX.X),
             reads=[gp_b], writes=[gcol_b], tag=f"gm1{t}")
        p.op("vector", lambda v: v.tensor_single_scalar(
            gtmp8[:], gp[:], gcol[:, 2:3], ALU.is_ge),
            reads=[gp_b, gcol_b], writes=[gtmp8_b], tag=f"geq{t}")
        p.op("vector", lambda v: v.scalar_tensor_tensor(
            gtmp8[:], gtmp8[:], BIG, gp[:], ALU.mult, ALU.subtract),
            reads=[gtmp8_b, gp_b], writes=[gtmp8_b], tag=f"gp2{t}")
        p.op("vector", lambda v: v.tensor_reduce(gcol[:, 2:3], gtmp8[:],
                                                 AX.X, ALU.min),
             reads=[gtmp8_b], writes=[gcol_b], tag=f"gm2{t}")
        p.op("vector", lambda v: v.tensor_scalar(
            gtmp8[:], gp[:], gcol[:, 2:3], 0.0, ALU.add, ALU.is_ge),
            reads=[gp_b, gcol_b], writes=[gtmp8_b], tag=f"gmk{t}")
        p.op("vector", lambda v: v.tensor_mul(gp[:], gp[:], gtmp8[:]),
             reads=[gp_b, gtmp8_b], writes=[gp_b], tag=f"gm{t}")
        p.op("tensor", lambda te: te.transpose(
            ps[3][0][0:8, 0:128], gp[:], ident_f[:]),
            reads=[gp_b, ident_f_bb], writes=[ps[3][1]], tag=f"gmt{t}")
        p.op("vector", lambda v, t=t: v.tensor_copy(
            m8[:, t * 128:(t + 1) * 128], ps[3][0][0:8, 0:128]),
            reads=[ps[3][1]], writes=[m8_b], tag=f"gme{t}")
        p.op("vector", lambda v, t=t: v.tensor_single_scalar(
            gtmp8[:], gp[:], rstd_cols[:, t:t + 1], ALU.mult),
            reads=[gp_b, rstd_cols_b], writes=[gtmp8_b], tag=f"gmr{t}")
        p.op("tensor", lambda te: te.transpose(
            ps[3][0][0:8, 0:128], gtmp8[:], ident_f[:]),
            reads=[gtmp8_b, ident_f_bb], writes=[ps[3][1]], tag=f"grt{t}")
        p.op("vector", lambda v, t=t: v.tensor_copy(
            mr8[:, t * 128:(t + 1) * 128], ps[3][0][0:8, 0:128]),
            reads=[ps[3][1]], writes=[mr8_b], tag=f"gre{t}")

    p.barrier()
    es_st.close()
    ps2 = [ctx.psum(f"psx{i}", [128, 512], F32) for i in range(4)]
    pool = ps + ps2

    out_toks = []
    for e in range(E):
        for j in range(0, T, 512):
            pi = pool[(j // 512) % 4]

            def em(te, e=e, j=j, pi=pi):
                return te.matmul(pi[0][:], sel8[:, e * 128:(e + 1) * 128],
                                 mr8[:, j:j + 512], start=True, stop=True)
            p.op("tensor", em, reads=[mr8_b, sel8_b], writes=[pi[1]],
                 tag=f"mb{e}_{j}")
            p.op("vector", lambda v, j=j, pi=pi: v.tensor_copy(
                mrbc[:, j:j + 512], pi[0][:]),
                reads=[pi[1]], writes=[mrbc_b], tag=f"mbe{e}_{j}")
        p.dma("gpsimd", out=b1row[:], in_=b1r[e:e + 1, :], writes=[b1row_b],
              tag=f"b1{e}")

        for th in range(2):
            toff = th * TH
            for hc in range(HCH):
                bi = hc % 2
                p.dma("sync", out=w1blk[:, bi * 1024:(bi + 1) * 1024],
                      in_=w1bf[(e * HCH + hc) * 128:(e * HCH + hc + 1) * 128, :],
                      writes=[w1bufs[bi]], tag=f"w1l{e}_{th}_{hc}")
                pp = (pool[0], pool[1]) if hc % 2 == 0 else (pool[2], pool[3])

                def em_h(te, hc=hc, toff=toff, bi=bi, pp=pp, e=e):
                    for c in range(DCH):
                        for j in range(2):
                            te.matmul(pp[j][0][:],
                                      w1blk[:, bi * 1024 + c * 128:
                                            bi * 1024 + (c + 1) * 128],
                                      x_bf[:, c * T + toff + j * 512:
                                           c * T + toff + (j + 1) * 512],
                                      start=(c == 0), stop=False)
                    last = None
                    for j in range(2):
                        last = te.matmul(pp[j][0][:],
                                         b1row[0:1, hc * 128:(hc + 1) * 128],
                                         stdr[0:1, toff + j * 512:
                                              toff + (j + 1) * 512],
                                         start=False, stop=True)
                    return last
                p.op("tensor", em_h,
                     reads=xbufs + [w1bufs[bi], b1row_b, stdr_b],
                     writes=[pp[0][1], pp[1][1]], tag=f"h{e}_{th}_{hc}")
                for j in range(2):
                    def em_ev(v, hc=hc, j=j, toff=toff, pp=pp):
                        return v.scalar_tensor_tensor(
                            h_sb[:, hc * TH + j * 512: hc * TH + (j + 1) * 512],
                            pp[j][0][:], 0.0,
                            mrbc[:, toff + j * 512: toff + (j + 1) * 512],
                            ALU.max, ALU.mult)
                    p.op("vector", em_ev, reads=[pp[j][1], mrbc_b],
                         writes=[hbufs[hc]], tag=f"he{e}_{th}_{hc}_{j}")
            for dc in range(DCH):
                bi = dc % 2
                p.dma("sync", out=w2col[:, bi * 4096:(bi + 1) * 4096],
                      in_=w2bf[(e * 8 + dc) * 128:(e * 8 + dc + 1) * 128, :],
                      writes=[w2bufs[bi]], tag=f"w2l{e}_{th}_{dc}")
                for j in range(2):
                    pi = pool[4 + (dc % 2) * 2 + j]

                    def em_o(te, dc=dc, j=j, bi=bi, e=e, toff=toff, pi=pi):
                        last = None
                        for hc in range(HCH):
                            last = te.matmul(pi[0][:],
                                             w2col[:, bi * 4096 + hc * 128:
                                                   bi * 4096 + (hc + 1) * 128],
                                             h_sb[:, hc * TH + j * 512:
                                                  hc * TH + (j + 1) * 512],
                                             start=(hc == 0),
                                             stop=(e > 0 and hc == HCH - 1))
                        if e == 0:
                            last = te.matmul(pi[0][:],
                                             b2sb[0:8, dc * 128:(dc + 1) * 128],
                                             m8[0:8, toff + j * 512:
                                                toff + (j + 1) * 512],
                                             start=False, stop=True)
                        return last
                    p.op("tensor", em_o,
                         reads=hbufs + [w2bufs[bi], b2sb_b, m8_b],
                         writes=[pi[1]], tag=f"o{e}_{th}_{dc}_{j}")
                    oi = (dc * 2 + j) % 2

                    def em_oc(v, pi=pi, oi=oi):
                        return v.tensor_copy(osb[:, oi * 512:(oi + 1) * 512],
                                             pi[0][:])
                    p.op("vector", em_oc, reads=[pi[1]],
                         writes=[osb_bufs[oi]], tag=f"oc{e}_{th}_{dc}_{j}")
                    tok = p.dma("gpsimd",
                                out=yacc[dc * 128:(dc + 1) * 128,
                                         toff + j * 512: toff + (j + 1) * 512],
                                in_=osb[:, oi * 512:(oi + 1) * 512],
                                reads=[osb_bufs[oi]], accum=(e > 0),
                                tag=f"ya{e}_{th}_{dc}_{j}")
                    if e == E - 1:
                        out_toks.append(tok)

    p.barrier()
    for dc in range(DCH):
        for j in range(4):
            qoff = j * 512
            p.dma("sync", out=xres[:],
                  in_=xT[dc * 128:(dc + 1) * 128, qoff:qoff + 512],
                  writes=[xres_b], tag=f"fx{dc}_{j}")
            p.dma("sync", out=yin[:],
                  in_=yacc[dc * 128:(dc + 1) * 128, qoff:qoff + 512],
                  writes=[yin_b], tag=f"fy{dc}_{j}")
            yi = j % 2

            def em_a(v, yi=yi):
                return v.tensor_add(ysb[:, yi * 512:(yi + 1) * 512], yin[:],
                                    xres[:])
            p.op("vector", em_a, reads=[yin_b, xres_b], writes=[ysb_bufs[yi]],
                 tag=f"fa{dc}_{j}")
            tok = p.dma("sync", out=yT[dc * 128:(dc + 1) * 128, qoff:qoff + 512],
                        in_=ysb[:, yi * 512:(yi + 1) * 512],
                        reads=[ysb_bufs[yi]], tag=f"fst{dc}_{j}")
            out_toks.append(tok)

    p.finish_outputs(out_toks[-1:])
    p.emit(es)
    es.close()
    return nc


def host_moe_inmaps(X, inputs, wbf):
    consts = host_consts()
    sel8 = np.zeros((8, 1024), BF)
    for e in range(8):
        sel8[e, e * 128:(e + 1) * 128] = 1
    shared = dict(gwT=_ct(np.asarray(inputs["L1_ffn_gate_w"], np.float32).T),
                  b1r=_ct(np.asarray(inputs["L1_ffn_b1"], np.float32)),
                  b2r=_ct(np.asarray(inputs["L1_ffn_b2"], np.float32)),
                  sel8=sel8,
                  fconsts=host_fconsts(FCONST_VALUES),
                  **{k: consts[k] for k in ["ones_col_bf", "ones_row_f",
                                            "ones_row_bf", "ident_f"]})
    w1 = wbf["w1bf"].reshape(8 * 32 * 128, 8 * 128)
    w2 = wbf["w2bf"].reshape(8 * 8 * 128, 32 * 128)
    return [dict(xT=X[c], w1bf=w1, w2bf=w2, **shared) for c in range(NCORES)]


def run_moe(X, inputs, wbf):
    if "moe" not in _CACHE:
        _CACHE["moe"] = build_moe()
    im = host_moe_inmaps(X, inputs, wbf)
    res = run_bass_kernel_spmd(_CACHE["moe"], im, core_ids=list(range(NCORES)))
    return [res.results[c]["yT"] for c in range(NCORES)]



def emit_stats_and_ln(ctx, consts, rs, pref, x_sb, xbufs, T, cs, psR1, psR1_b,
                      psR2, psR2_b, x2, x2_b, negmean, negmean_b, rstdf,
                      rstdf_b, nmr, nmr_b, bc, bc_b, ps, roff=0, coff=0):
    p = ctx.p
    ones_bf, ones_bf_bb = consts["ones_col_bf"]
    ones_row, ones_row_bb = consts["ones_row_f"]
    QW = 1024
    for q0 in range(0, T, QW):
        w = min(QW, T - q0)
        for c in range(DCH):
            xs = x_sb[:, c * cs + coff + q0: c * cs + coff + q0 + w]

            def em_sum(te, xs=xs, c=c, w=w):
                last = None
                for j in range(0, w, 512):
                    wj = min(512, w - j)
                    last = te.matmul(psR1[:, j:j + wj], ones_bf[:],
                                     xs[:, j:j + wj], start=(c == 0),
                                     stop=(c == DCH - 1))
                return last
            p.op("tensor", em_sum, reads=[xbufs[c], ones_bf_bb],
                 writes=[psR1_b], tag=f"{pref}sum{q0}_{c}")

            def em_sq(s, xs=xs, w=w):
                return s.activation(out=x2[:, 0:w], in_=xs, func=AF.Square)
            p.op("scalar", em_sq, reads=[xbufs[c]], writes=[x2_b],
                 tag=f"{pref}sq{q0}_{c}")

            def em_sumq(te, c=c, w=w):
                last = None
                for j in range(0, w, 512):
                    wj = min(512, w - j)
                    last = te.matmul(psR2[:, j:j + wj], ones_bf[:],
                                     x2[:, j:j + wj], start=(c == 0),
                                     stop=(c == DCH - 1))
                return last
            p.op("tensor", em_sumq, reads=[x2_b, ones_bf_bb], writes=[psR2_b],
                 tag=f"{pref}sumq{q0}_{c}")
        emit_row_stats(ctx, f"{pref}st{q0}", w, D, rs, psR1, psR1_b, psR2,
                       psR2_b, negmean[0:1, roff + q0:roff + q0 + w],
                       negmean_b, rstdf[0:1, roff + q0:roff + q0 + w],
                       rstdf_b)
        def em_nmr(v, q0=q0, w=w):
            return v.tensor_mul(nmr[0:1, 0:w], rs["nmf"][0][0:1, 0:w],
                                rstdf[0:1, roff + q0:roff + q0 + w])
        p.op("vector", em_nmr, reads=[rs["nmf"][1], rstdf_b], writes=[nmr_b],
             tag=f"{pref}nmr{q0}")
        for j in range(0, w, 512):
            wj = min(512, w - j)

            def em_rb(te, q0=q0, j=j, wj=wj):
                return te.matmul(ps[0][0][:, 0:wj], ones_row[:],
                                 rstdf[0:1, roff + q0 + j:
                                       roff + q0 + j + wj],
                                 start=True, stop=True)
            p.op("tensor", em_rb, reads=[rstdf_b, ones_row_bb],
                 writes=[ps[0][1]], tag=f"{pref}rb{q0}_{j}")
            p.op("vector", lambda v, wj=wj: v.tensor_copy(bc[:, 0:wj],
                                                          ps[0][0][:, 0:wj]),
                 reads=[ps[0][1]], writes=[bc_b], tag=f"{pref}rbe{q0}_{j}")

            def em_nb(te, j=j, wj=wj):
                return te.matmul(ps[1][0][:, 0:wj], ones_row[:],
                                 nmr[0:1, j:j + wj], start=True, stop=True)
            p.op("tensor", em_nb, reads=[nmr_b, ones_row_bb],
                 writes=[ps[1][1]], tag=f"{pref}nb{q0}_{j}")
            p.op("vector", lambda v, wj=wj: v.tensor_copy(
                bc[:, 512:512 + wj], ps[1][0][:, 0:wj]),
                reads=[ps[1][1]], writes=[bc_b], tag=f"{pref}nbe{q0}_{j}")
            for c in range(DCH):
                xs = x_sb[:, c * cs + coff + q0 + j:
                          c * cs + coff + q0 + j + wj]

                def em_m(v, xs=xs, wj=wj):
                    return v.tensor_mul(xs, xs, bc[:, 0:wj])
                p.op("vector", em_m, reads=[xbufs[c], bc_b],
                     writes=[xbufs[c]], tag=f"{pref}lm{q0}_{j}_{c}")

                def em_a(v, xs=xs, wj=wj):
                    return v.tensor_add(xs, xs, bc[:, 512:512 + wj])
                p.op("vector", em_a, reads=[xbufs[c], bc_b],
                     writes=[xbufs[c]], tag=f"{pref}la{q0}_{j}_{c}")



def build_attnb():
    nc = bass.Bass("TRN2", target_bir_lowering=False, debug=False)
    T = TLOC
    TK = L
    NH = 16
    HD = 64

    xqT = nc.dram_tensor("xqT", [D, T], F32, kind="ExternalInput").ap()
    xkT = nc.dram_tensor("xkT", [D, TK], F32, kind="ExternalInput").ap()
    wqkvt = nc.dram_tensor("wqkvt", [16 * 128, 8 * 128], F32,
                           kind="ExternalInput").ap()
    wvT = nc.dram_tensor("wvT", [D, D], F32, kind="ExternalInput").ap()
    woutT = nc.dram_tensor("woutT", [D, D], F32, kind="ExternalInput").ap()
    brow = nc.dram_tensor("brow", [1, 3072], F32, kind="ExternalInput").ap()
    borow = nc.dram_tensor("borow", [1, D], F32, kind="ExternalInput").ap()
    yT = nc.dram_tensor("yT", [D, T], F32, kind="ExternalOutput").ap()
    ks = nc.dram_tensor("ks", [D, TK], BF16)
    if DEBUG:
        dq = nc.dram_tensor("dq", [128, 2048], F32, kind="ExternalOutput").ap()
        dk = nc.dram_tensor("dk", [128, 4096], F32, kind="ExternalOutput").ap()
        dv = nc.dram_tensor("dv", [128, 1040], F32, kind="ExternalOutput").ap()
        do_ = nc.dram_tensor("do_", [128, 1024], F32, kind="ExternalOutput").ap()
        dxl = nc.dram_tensor("dxl", [128, 2048], F32, kind="ExternalOutput").ap()
        dq2 = nc.dram_tensor("dq2", [128, 2048], BF16, kind="ExternalOutput").ap()

    es = ExitStack()
    p = Prog(nc)
    ctx = Ctx(nc, es, p)
    consts = load_consts(ctx, ["ones_col_bf", "ones_row_f", "ones_row_bf",
                               "ones_row512"])
    load_fconsts(ctx, FCONST_VALUES)
    ones_bf, ones_bf_bb = consts["ones_col_bf"]
    ones_row, ones_row_bb = consts["ones_row_f"]
    ones_row_b16, ones_row_b16_bb = consts["ones_row_bf"]
    ones512, ones512_bb = consts["ones_row512"]

    negm_s, negm_s_b = ctx.sbuf("negm_s", [1, T], BF16)
    rstd_s, rstd_s_b = ctx.sbuf("rstd_s", [1, T], F32)
    nmr, nmr_b = ctx.sbuf("nmr", [1, 1024], F32)
    bc, bc_b = ctx.sbuf("bc", [128, 1024], F32)
    x2, x2_b = ctx.sbuf("x2", [128, 1024], BF16)
    rs = make_row_scratch(ctx)
    brow_sb, brow_sb_b = ctx.sbuf("brow_sb", [1, 3072], BF16)
    borow_sb, borow_sb_b = ctx.sbuf("borow_sb", [1, D], BF16)
    rdbf, rdbf_b = ctx.sbuf("rdbf", [1, 1024], BF16)
    rdbc, rdbc_b = ctx.sbuf("rdbc", [128, 1024], BF16)
    wbuf, _wb = ctx.sbuf("wbuf", [128, 2 * 1024], BF16)
    wbufs = [Buf("wb0"), Buf("wb1")]
    expt, _ex = ctx.sbuf("expt", [128, 2 * 1024], BF16)
    expt_bufs = [Buf("ex0"), Buf("ex1")]
    xres, xres_b = ctx.sbuf("xres", [128, 512], F32)
    ysb, _ys = ctx.sbuf("ysb", [128, 2 * 512], F32)
    ysb_bufs = [Buf("ysb0"), Buf("ysb1")]

    q_sb, _ = ctx.sbuf("q_sb", [128, 8 * T], BF16)
    qbufs = [Buf(f"q{pk}") for pk in range(8)]
    k_sb, _ = ctx.sbuf("k_sb", [128, 2 * TK], BF16)
    kbufs = [Buf("k0"), Buf("k1")]
    kev, kev_b = ctx.sbuf("kev", [128, 512], BF16)
    v_sb, _ = ctx.sbuf("v_sb", [128, 32 * 1040], BF16)
    vbufs = [Buf(f"v{t}") for t in range(32)]
    vre = v_sb.rearrange("p (t h c) -> p t h c", t=32, h=16)

    es_pst = ExitStack()
    psS = [ctx.psum(f"psS{i}", [128, 1024], F32) for i in range(2)]
    psR1, psR1_b = ctx.psum("psR1", [1, 1024], F32, es=es_pst)
    psR2, psR2_b = ctx.psum("psR2", [1, 1024], F32, es=es_pst)
    psLN = [(psS[0][0][:, 0:512], psS[0][1]), (psS[1][0][:, 0:512], psS[1][1])]

    p.dma("gpsimd", out=brow_sb[:], in_=brow, writes=[brow_sb_b], tag="lbr")
    p.dma("gpsimd", out=borow_sb[:], in_=borow, writes=[borow_sb_b], tag="lbo")

    es_xq = ExitStack()
    xq_bf, _ = ctx.sbuf("xq_bf", [128, DCH * T], BF16, es=es_xq)
    xqbufs = [Buf(f"xq{c}") for c in range(DCH)]
    for c in range(DCH):
        p.dma("gpsimd", out=xq_bf[:, c * T:(c + 1) * T],
              in_=xqT[c * 128:(c + 1) * 128, :], writes=[xqbufs[c]],
              tag=f"lxq{c}")
    emit_stats_and_ln(ctx, consts, rs, "q", xq_bf, xqbufs, T, T, psR1, psR1_b,
                      psR2, psR2_b, x2, x2_b, negm_s, negm_s_b, rstd_s,
                      rstd_s_b, nmr, nmr_b, bc, bc_b, psLN)
    if DEBUG:
        p.dma("gpsimd", out=dxl, in_=xq_bf[:, 5 * T:5 * T + 2048],
              reads=[xqbufs[5]], tag="dbxl")
    for pk in range(8):
        bi = pk % 2
        p.dma("gpsimd", out=wbuf[:, bi * 1024:(bi + 1) * 1024],
              in_=wqkvt[pk * 128:(pk + 1) * 128, :], writes=[wbufs[bi]],
              tag=f"wq{pk}")
        for jr in range(0, T, 512):
            pi = psS[(jr // 512) % 2]

            def em_q(te, pk=pk, bi=bi, jr=jr, pi=pi):
                for c in range(DCH):
                    te.matmul(pi[0][:, 0:512],
                              wbuf[:, bi * 1024 + c * 128:
                                   bi * 1024 + (c + 1) * 128],
                              xq_bf[:, c * T + jr: c * T + jr + 512],
                              start=(c == 0), stop=False)
                return te.matmul(pi[0][:, 0:512],
                                 brow_sb[0:1, pk * 128:(pk + 1) * 128],
                                 ones512[:], start=False, stop=True)
            p.op("tensor", em_q, reads=xqbufs + [wbufs[bi], brow_sb_b,
                                                 ones512_bb],
                 writes=[pi[1]], tag=f"q{pk}_{jr}")
            p.op("vector", lambda v, pk=pk, jr=jr, pi=pi: v.tensor_copy(
                q_sb[:, pk * T + jr: pk * T + jr + 512], pi[0][:, 0:512]),
                reads=[pi[1]], writes=[qbufs[pk]], tag=f"qe{pk}_{jr}")
    p.barrier()
    if DEBUG:
        p.dma("gpsimd", out=dq, in_=q_sb[:, 0:2048], reads=[qbufs[0]],
              tag="dbq")
        p.dma("sync", out=dq2, in_=q_sb[:, 0:2048], reads=[qbufs[0]],
              tag="dbq2")
    es_xq.close()

    es_xk = ExitStack()
    TS = 1024
    xk_bf, _ = ctx.sbuf("xk_bf", [128, DCH * TS], BF16, es=es_xk)
    xkbufs = [Buf(f"xk{c}") for c in range(DCH)]
    wv_sb, _wv = ctx.sbuf("wv_sb", [128, 8 * 1024], BF16, es=es_xk)
    wv_bufs = [Buf(f"wv{c}") for c in range(DCH)]
    for c in range(DCH):
        p.dma("gpsimd", out=wv_sb[:, c * 1024:(c + 1) * 1024],
              in_=wvT[c * 128:(c + 1) * 128, :], writes=[wv_bufs[c]],
              tag=f"wvl{c}")

    vre3 = v_sb.rearrange("p (th c) -> p th c", c=65)

    def em_vones(v):
        return v.memset(vre3[:, :, 64:65], 1.0)
    p.op("vector", em_vones, writes=vbufs, tag="vones")

    for st in range(4):
        soff = st * TS
        for c in range(DCH):
            p.dma("gpsimd", out=xk_bf[:, c * TS:(c + 1) * TS],
                  in_=xkT[c * 128:(c + 1) * 128, soff:soff + TS],
                  writes=[xkbufs[c]], tag=f"lxk{st}_{c}")
        emit_stats_and_ln(ctx, consts, rs, f"k{st}", xk_bf, xkbufs, TS, TS,
                          psR1, psR1_b, psR2, psR2_b, x2, x2_b, negm_s,
                          negm_s_b, rstd_s, rstd_s_b, nmr, nmr_b, bc, bc_b,
                          psLN)
        for pk in range(8):
            bi = pk % 2
            p.dma("gpsimd", out=wbuf[:, bi * 1024:(bi + 1) * 1024],
                  in_=wqkvt[(8 + pk) * 128:(9 + pk) * 128, :],
                  writes=[wbufs[bi]], tag=f"wk{st}_{pk}")
            for jr in range(0, TS, 512):
                pi = psS[(jr // 512) % 2]

                def em_k(te, pk=pk, bi=bi, jr=jr, pi=pi):
                    for c in range(DCH):
                        te.matmul(pi[0][:, 0:512],
                                  wbuf[:, bi * 1024 + c * 128:
                                       bi * 1024 + (c + 1) * 128],
                                  xk_bf[:, c * TS + jr: c * TS + jr + 512],
                                  start=(c == 0), stop=False)
                    return te.matmul(pi[0][:, 0:512],
                                     brow_sb[0:1, 1024 + pk * 128:
                                             1024 + (pk + 1) * 128],
                                     ones512[:], start=False, stop=True)
                p.op("tensor", em_k, reads=xkbufs + [wbufs[bi], brow_sb_b,
                                                     ones512_bb],
                     writes=[pi[1]], tag=f"k{st}_{pk}_{jr}")
                p.op("vector", lambda v, pi=pi: v.tensor_copy(
                    kev[:], pi[0][:, 0:512]),
                    reads=[pi[1]], writes=[kev_b], tag=f"ke{st}_{pk}_{jr}")
                p.dma("sync", out=ks[pk * 128:(pk + 1) * 128,
                                    soff + jr: soff + jr + 512],
                      in_=kev[:], reads=[kev_b], tag=f"ks{st}_{pk}_{jr}")
        for jt in range(st * 8, (st + 1) * 8):
            jloc = (jt - st * 8) * 128
            for g in range(2):
                pi = psS[g]

                def em_v(te, jloc=jloc, g=g, pi=pi):
                    for c in range(DCH):
                        te.matmul(pi[0][:, 0:512],
                                  xk_bf[:, c * TS + jloc: c * TS + jloc + 128],
                                  wv_sb[:, c * 1024 + g * 512:
                                        c * 1024 + (g + 1) * 512],
                                  start=(c == 0), stop=False)
                    return te.matmul(pi[0][:, 0:512], ones_row_b16[:],
                                     brow_sb[0:1, 2048 + g * 512:
                                             2048 + (g + 1) * 512],
                                     start=False, stop=True)
                p.op("tensor", em_v, reads=xkbufs + wv_bufs +
                     [brow_sb_b, ones_row_b16_bb],
                     writes=[pi[1]], tag=f"v{jt}_{g}")

                def em_ve(v, jt=jt, g=g, pi=pi):
                    return v.tensor_copy(vre[:, jt, g * 8:(g + 1) * 8, 0:64],
                                         pi[0][:, 0:512])
                p.op("vector", em_ve, reads=[pi[1]], writes=[vbufs[jt]],
                     tag=f"ve{jt}_{g}")
    p.barrier()
    if DEBUG:
        p.dma("gpsimd", out=dv, in_=v_sb[:, 0:1040], reads=[vbufs[0]],
              tag="dbv")
        p.dma("gpsimd", out=dk, in_=ks[0:128, :], tag="dbk")
    if S3_TRUNC >= 6:
        tok = p.dma("sync", out=yT[0:128, 0:512], in_=ysb[:, 0:512],
                    tag="dummy6")
        p.finish_outputs([tok])
        p.emit(es)
        es_xk.close()
        es_pst.close()
        es.close()
        return nc
    es_xk.close()
    es_pst.close()
    psA = [ctx.psum(f"psA{i}", [65, 1024], F32) for i in range(2)]
    wosb, _wo = ctx.sbuf("wosb", [128, 8 * 1024], BF16)
    wo_bufs = [Buf(f"wo{c}") for c in range(DCH)]
    for c in range(DCH):
        p.dma("gpsimd", out=wosb[:, c * 1024:(c + 1) * 1024],
              in_=woutT[c * 128:(c + 1) * 128, :], writes=[wo_bufs[c]],
              tag=f"lwo{c}")
    o_sb, _ = ctx.sbuf("o_sb", [128, 8 * 1024], BF16)
    obufs = [Buf(f"o{pk}") for pk in range(8)]
    if S3_TRUNC >= 5:
        tok = p.dma("sync", out=yT[0:128, 0:512], in_=ysb[:, 0:512],
                    tag="dummy5")
        p.finish_outputs([tok])
        p.emit(es)
        es.close()
        return nc


    NIR = 2
    IW = T // NIR
    tok = None
    for ir in range(NIR):
        i0 = ir * IW
        for pk in range(8):
            kbi = pk % 2
            p.dma("gpsimd", out=k_sb[:, kbi * TK:(kbi + 1) * TK],
                  in_=ks[pk * 128:(pk + 1) * 128, :], writes=[kbufs[kbi]],
                  tag=f"kld{ir}_{pk}")
            for jt in range(32):
                sbi = jt % 2

                def em_s(te, pk=pk, jt=jt, i0=i0, kbi=kbi):
                    last = None
                    for h2 in range(2):
                        lhs = k_sb[h2 * 64:(h2 + 1) * 64,
                                   kbi * TK + jt * 128:
                                   kbi * TK + (jt + 1) * 128]
                        for j in range(2):
                            last = te.matmul(
                                psS[h2][0][:, j * 512:(j + 1) * 512],
                                lhs,
                                q_sb[h2 * 64:(h2 + 1) * 64,
                                     pk * T + i0 + j * 512:
                                     pk * T + i0 + (j + 1) * 512],
                                start=True, stop=True)
                    return last
                if S3_TRUNC < 4:
                    p.op("tensor", em_s, reads=[kbufs[kbi], qbufs[pk]],
                         writes=[psS[0][1], psS[1][1]], tag=f"s{ir}_{pk}_{jt}")
                for h2 in range(2):
                    ebi = (2 * jt + h2) % 2

                    def em_e(s, h2=h2, ebi=ebi):
                        return s.activation(
                            out=expt[:, ebi * 1024: ebi * 1024 + 1024],
                            in_=psS[h2][0][:], func=AF.Exp, scale=0.125)
                    if S3_TRUNC < 3:
                        p.op("scalar", em_e, reads=[psS[h2][1]],
                             writes=[expt_bufs[ebi]], tag=f"e{ir}_{pk}_{jt}_{h2}")

                    def em_av(te, pk=pk, jt=jt, h2=h2, ebi=ebi):
                        last = None
                        for j in range(2):
                            last = te.matmul(
                                psA[h2][0][:, j * 512:(j + 1) * 512],
                                vre[:, jt, pk * 2 + h2, :],
                                expt[:, ebi * 1024 + j * 512:
                                     ebi * 1024 + (j + 1) * 512],
                                start=(jt == 0), stop=(jt == 31))
                        return last
                    if S3_TRUNC < 2:
                        p.op("tensor", em_av, reads=[vbufs[jt], expt_bufs[ebi]],
                             writes=[psA[h2][1]], tag=f"av{ir}_{pk}_{jt}_{h2}")
            for h2 in range(2 if S3_TRUNC < 2 else 0):
                p.op("vector", lambda v, h2=h2: v.reciprocal(
                    nmr[0:1, 0:1024], psA[h2][0][64:65, :]),
                    reads=[psA[h2][1]], writes=[nmr_b],
                    tag=f"rd{ir}_{pk}_{h2}")
                p.op("vector", lambda v: v.tensor_copy(rdbf[0:1, 0:1024],
                                                       nmr[0:1, 0:1024]),
                     reads=[nmr_b], writes=[rdbf_b], tag=f"rdb{ir}_{pk}_{h2}")

                def em_rb(te):
                    last = None
                    for j in range(2):
                        last = te.matmul(
                            psS[0][0][0:64, j * 512:(j + 1) * 512],
                            ones512[0:1, 0:64],
                            rdbf[0:1, j * 512:(j + 1) * 512],
                            start=True, stop=True)
                    return last
                p.op("tensor", em_rb, reads=[rdbf_b, ones512_bb],
                     writes=[psS[0][1]], tag=f"rbb{ir}_{pk}_{h2}")
                p.op("vector", lambda v: v.tensor_copy(rdbc[0:64, 0:1024],
                                                       psS[0][0][0:64, :]),
                     reads=[psS[0][1]], writes=[rdbc_b],
                     tag=f"rbe{ir}_{pk}_{h2}")

                def em_oe(v, pk=pk, h2=h2):
                    return v.tensor_mul(
                        o_sb[h2 * 64:(h2 + 1) * 64,
                             pk * 1024: pk * 1024 + 1024],
                        psA[h2][0][0:64, :], rdbc[0:64, 0:1024])
                p.op("vector", em_oe, reads=[psA[h2][1], rdbc_b],
                     writes=[obufs[pk]], tag=f"oe{ir}_{pk}_{h2}")
        for j2 in range(0, IW, 512) if S3_TRUNC < 1 else []:
            for dc in range(DCH):
                pi = psS[dc % 2]

                def em_w(te, dc=dc, j2=j2, pi=pi):
                    for ec in range(8):
                        te.matmul(pi[0][:, 0:512],
                                  wosb[:, ec * 1024 + dc * 128:
                                       ec * 1024 + (dc + 1) * 128],
                                  o_sb[:, ec * 1024 + j2:
                                       ec * 1024 + j2 + 512],
                                  start=(ec == 0), stop=False)
                    return te.matmul(pi[0][:, 0:512],
                                     borow_sb[0:1, dc * 128:(dc + 1) * 128],
                                     ones512[:], start=False, stop=True)
                p.op("tensor", em_w, reads=obufs + wo_bufs +
                     [borow_sb_b, ones512_bb], writes=[pi[1]],
                     tag=f"w{ir}_{j2}_{dc}")
                p.dma("sync", out=xres[:],
                      in_=xqT[dc * 128:(dc + 1) * 128,
                              i0 + j2: i0 + j2 + 512],
                      writes=[xres_b], tag=f"wx{ir}_{j2}_{dc}")
                yi = dc % 2

                def em_y(v, pi=pi, yi=yi):
                    return v.tensor_add(ysb[:, yi * 512:(yi + 1) * 512],
                                        pi[0][:, 0:512], xres[:])
                p.op("vector", em_y, reads=[pi[1], xres_b],
                     writes=[ysb_bufs[yi]], tag=f"wy{ir}_{j2}_{dc}")
                tok = p.dma("sync",
                            out=yT[dc * 128:(dc + 1) * 128,
                                   i0 + j2: i0 + j2 + 512],
                            in_=ysb[:, yi * 512:(yi + 1) * 512],
                            reads=[ysb_bufs[yi]], tag=f"wst{ir}_{j2}_{dc}")
    if S3_TRUNC >= 1:
        tok = p.dma("sync", out=yT[0:128, 0:512], in_=ysb[:, 0:512], tag="dummy")
    if DEBUG:
        p.barrier()
        p.dma("gpsimd", out=do_, in_=o_sb[:, 0:1024], reads=[obufs[0]],
              tag="dbo")
        p.barrier()
    p.finish_outputs([tok])
    p.emit(es)
    es.close()
    return nc


def host_attnb_inmaps(X, inputs, lname):
    consts = host_consts()
    wqkv = np.asarray(inputs[f"{lname}_W_qkv"], np.float32)
    wqkvt = _ct(wqkv[:, 0:2048].reshape(8, 128, 16, 128)
                .transpose(2, 1, 0, 3).reshape(16 * 128, 8 * 128))
    wvT = _ct(wqkv[:, 2048:3072])
    shared = dict(wqkvt=wqkvt, wvT=wvT,
                  woutT=_ct(np.asarray(inputs[f"{lname}_W_out"], np.float32)),
                  brow=_ct(np.asarray(inputs[f"{lname}_b_qkv"],
                                      np.float32)[None, :]),
                  borow=_ct(np.asarray(inputs[f"{lname}_b_out"],
                                       np.float32)[None, :]),
                  fconsts=host_fconsts(FCONST_VALUES),
                  **{k: consts[k] for k in ["ones_col_bf", "ones_row_f",
                                            "ones_row_bf", "ones_row512"]})
    ims = []
    for c in range(NCORES):
        b = c // 2
        xk = _ct(np.concatenate([X[2 * b], X[2 * b + 1]], axis=1))
        ims.append(dict(xqT=X[c], xkT=xk, **shared))
    return ims


def run_attnb(X, inputs, lname):
    if "attnb" not in _CACHE:
        _CACHE["attnb"] = build_attnb()
    im = host_attnb_inmaps(X, inputs, lname)
    res = run_bass_kernel_spmd(_CACHE["attnb"], im,
                               core_ids=list(range(NCORES)))
    return [res.results[c]["yT"] for c in range(NCORES)]




def build_mla():
    nc = bass.Bass("TRN2", target_bir_lowering=False, debug=False)
    T = TLOC
    TK = L
    NH = 16

    xqT = nc.dram_tensor("xqT", [D, T], F32, kind="ExternalInput").ap()
    xkT = nc.dram_tensor("xkT", [D, TK], F32, kind="ExternalInput").ap()
    wq = nc.dram_tensor("wq", [D, D], F32, kind="ExternalInput").ap()
    wkv = nc.dram_tensor("wkv", [D, 128], F32, kind="ExternalInput").ap()
    wk = nc.dram_tensor("wk", [128, 512], F32, kind="ExternalInput").ap()
    wv = nc.dram_tensor("wv", [128, 512], F32, kind="ExternalInput").ap()
    wosel = nc.dram_tensor("wosel", [512, D], F32, kind="ExternalInput").ap()
    gqt = nc.dram_tensor("gqt", [1, 512], F32, kind="ExternalInput").ap()
    gkt = nc.dram_tensor("gkt", [1, 512], F32, kind="ExternalInput").ap()
    vmask = nc.dram_tensor("vmask", [TK, T], BF16, kind="ExternalInput").ap()
    yT = nc.dram_tensor("yT", [D, T], F32, kind="ExternalOutput").ap()

    es = ExitStack()
    p = Prog(nc)
    ctx = Ctx(nc, es, p)
    consts = load_consts(ctx, ["ones_col_bf", "ones_row_f", "ones_row_bf",
                               "ones_row512", "ident_bf"])
    load_fconsts(ctx, FCONST_VALUES)
    ones_bf, ones_bf_bb = consts["ones_col_bf"]
    ones_row, ones_row_bb = consts["ones_row_f"]
    ones512, ones512_bb = consts["ones_row512"]
    ident_bf, ident_bf_bb = consts["ident_bf"]

    negm_s, negm_s_b = ctx.sbuf("negm_s", [1, T], BF16)
    rstd_s, rstd_s_b = ctx.sbuf("rstd_s", [1, T], F32)
    nmr, nmr_b = ctx.sbuf("nmr", [1, 1024], F32)
    bc, bc_b = ctx.sbuf("bc", [128, 1024], F32)
    x2, x2_b = ctx.sbuf("x2", [128, 1024], BF16)
    rs = make_row_scratch(ctx)
    gq_sb, gq_sb_b = ctx.sbuf("gq_sb", [1, 512], F32)
    gk_sb, gk_sb_b = ctx.sbuf("gk_sb", [1, 512], F32)
    gkv_row, gkv_row_b = ctx.sbuf("gkv_row", [1, 512], F32)
    gkv_bc, gkv_bc_b = ctx.sbuf("gkv_bc", [128, 512], BF16)
    gabs, gabs_b = ctx.sbuf("gabs", [1, 512], F32)
    gmax, gmax_b = ctx.sbuf("gmax", [1, 1], F32)
    negc, negc_b = ctx.sbuf("negc", [128, 1], F32)
    qsq, qsq_b = ctx.sbuf("qsq", [128, 64], F32)
    hss, hss_b = ctx.sbuf("hss", [128, 16], F32)
    hrstd, hrstd_b = ctx.sbuf("hrstd", [128, 16], F32)
    qh32, qh32_b = ctx.sbuf("qh32", [128, 512], BF16)
    rdbf, rdbf_b = ctx.sbuf("rdbf", [1, 1024], BF16)
    rdbc, rdbc_b = ctx.sbuf("rdbc", [32, 1024], BF16)
    mbuf, _mb = ctx.sbuf("mbuf", [128, 2 * 1024], BF16)
    mbufs = [Buf("mb0"), Buf("mb1")]
    expt, _ex = ctx.sbuf("expt", [128, 2 * 1024], BF16)
    expt_bufs = [Buf("ex0"), Buf("ex1")]
    xres, xres_b = ctx.sbuf("xres", [128, 512], F32)
    ysb, _ys = ctx.sbuf("ysb", [128, 2 * 512], F32)
    ysb_bufs = [Buf("ysb0"), Buf("ysb1")]
    wk_sb, wk_sb_b = ctx.sbuf("wk_sb", [128, 512], BF16)
    wv_sb, wv_sb_b = ctx.sbuf("wv_sb", [128, 512], BF16)

    q_fm, _ = ctx.sbuf("q_fm", [128, 4 * T], BF16)
    qbufs = [Buf(f"q{g}") for g in range(4)]
    k_fm, _ = ctx.sbuf("k_fm", [128, 4 * TK], BF16)
    kbufs = [Buf(f"k{g}") for g in range(4)]
    v_sb, _ = ctx.sbuf("v_sb", [128, 32 * 528], BF16)
    vbufs = [Buf(f"v{t}") for t in range(32)]
    vre = v_sb.rearrange("p (t h c) -> p t h c", t=32, h=16)
    vre3 = v_sb.rearrange("p (th c) -> p th c", c=33)

    es_ps = ExitStack()
    psG = [ctx.psum(f"psG{i}", [128, 512], F32, es=es_ps) for i in range(2)]
    psT, psT_b = ctx.psum("psT", [128, 128], BF16, es=es_ps)
    psR1, psR1_b = ctx.psum("psR1", [1, 1024], F32, es=es_ps)
    psR2, psR2_b = ctx.psum("psR2", [1, 1024], F32, es=es_ps)
    psLN = [(psG[0][0][:], psG[0][1]), (psG[1][0][:], psG[1][1])]

    p.dma("sync", out=gq_sb[:], in_=gqt, writes=[gq_sb_b], tag="lgq")
    p.dma("sync", out=gk_sb[:], in_=gkt, writes=[gk_sb_b], tag="lgk")
    p.op("vector", lambda v: v.tensor_mul(gkv_row[:], gq_sb[:], gk_sb[:]),
         reads=[gq_sb_b, gk_sb_b], writes=[gkv_row_b], tag="gkv")
    p.op("scalar", lambda s: s.activation(out=gabs[:], in_=gkv_row[:],
                                          func=AF.Abs),
         reads=[gkv_row_b], writes=[gabs_b], tag="gabs")
    p.op("vector", lambda v: v.reduce_max(gmax[:], gabs[:], axis=AX.X),
         reads=[gabs_b], writes=[gmax_b], tag="gmax")

    def em_cb(te):
        return te.matmul(psG[0][0][:, 0:1], ones_row[:], gmax[:], start=True,
                         stop=True)
    p.op("tensor", em_cb, reads=[gmax_b, ones_row_bb], writes=[psG[0][1]],
         tag="cbc")
    p.op("vector", lambda v: v.tensor_scalar_mul(negc[:], psG[0][0][:, 0:1],
                                                 -5.6569),
         reads=[psG[0][1]], writes=[negc_b], tag="negc")

    def em_gb(te):
        return te.matmul(psG[1][0][:], ones_row[:], gkv_row[:],
                         start=True, stop=True)
    p.op("tensor", em_gb, reads=[gkv_row_b, ones_row_bb], writes=[psG[1][1]],
         tag="gbc")
    p.op("vector", lambda v: v.tensor_copy(gkv_bc[:], psG[1][0][:]),
         reads=[psG[1][1]], writes=[gkv_bc_b], tag="gbe")
    p.dma("gpsimd", out=wk_sb[:], in_=wk, writes=[wk_sb_b], tag="lwk")
    p.dma("gpsimd", out=wv_sb[:], in_=wv, writes=[wv_sb_b], tag="lwv")

    def em_vones(v):
        return v.memset(vre3[:, :, 32:33], 1.0)
    p.op("vector", em_vones, writes=vbufs, tag="vones")

    es_xq = ExitStack()
    xq_bf, _ = ctx.sbuf("xq_bf", [128, DCH * T], BF16, es=es_xq)
    xqbufs = [Buf(f"xq{c}") for c in range(DCH)]
    wq_all, _wq = ctx.sbuf("wq_all", [128, DCH * 1024], BF16, es=es_xq)
    wq_bufs = [Buf(f"wq{c}") for c in range(DCH)]
    for c in range(DCH):
        p.dma("gpsimd", out=xq_bf[:, c * T:(c + 1) * T],
              in_=xqT[c * 128:(c + 1) * 128, :], writes=[xqbufs[c]],
              tag=f"lxq{c}")
        p.dma("gpsimd", out=wq_all[:, c * 1024:(c + 1) * 1024],
              in_=wq[c * 128:(c + 1) * 128, :], writes=[wq_bufs[c]],
              tag=f"lwq{c}")
    emit_stats_and_ln(ctx, consts, rs, "q", xq_bf, xqbufs, T, T, psR1, psR1_b,
                      psR2, psR2_b, x2, x2_b, negm_s, negm_s_b, rstd_s,
                      rstd_s_b, nmr, nmr_b, bc, bc_b, psLN)
    for t in range(16):
        def em_qt(te, t=t):
            last = None
            for g2 in range(2):
                for c in range(DCH):
                    last = te.matmul(psG[g2][0][:],
                                     xq_bf[:, c * T + t * 128:
                                           c * T + (t + 1) * 128],
                                     wq_all[:, c * 1024 + g2 * 512:
                                            c * 1024 + (g2 + 1) * 512],
                                     start=(c == 0), stop=(c == DCH - 1))
            return last
        p.op("tensor", em_qt, reads=xqbufs + wq_bufs,
             writes=[psG[0][1], psG[1][1]], tag=f"qt{t}")
        for h in range(NH):
            g2, off = h // 8, (h % 8) * 64

            def em_ss(s, g2=g2, off=off, h=h):
                return s.activation(out=qsq[:, 0:64],
                                    in_=psG[g2][0][:, off:off + 64],
                                    func=AF.Square,
                                    accum_out=hss[:, h:h + 1])
            p.op("scalar", em_ss, reads=[psG[g2][1]], writes=[qsq_b, hss_b],
                 tag=f"qss{t}_{h}")
        p.op("scalar", lambda s: s.activation(out=hrstd[:], in_=hss[:],
                                              func=AF.Ln, scale=0.015625,
                                              bias=1e-6),
             reads=[hss_b], writes=[hrstd_b], tag=f"qln{t}")
        p.op("scalar", lambda s: s.activation(out=hrstd[:], in_=hrstd[:],
                                              func=AF.Exp, scale=-0.5),
             reads=[hrstd_b], writes=[hrstd_b], tag=f"qex{t}")
        for h in range(NH):
            g2, off = h // 8, (h % 8) * 64

            def em_qn(v, g2=g2, off=off, h=h):
                return v.tensor_single_scalar(
                    qh32[:, h * 32:(h + 1) * 32],
                    psG[g2][0][:, off:off + 32], hrstd[:, h:h + 1], ALU.mult)
            p.op("vector", em_qn, reads=[psG[g2][1], hrstd_b],
                 writes=[qh32_b], tag=f"qn{t}_{h}")
        for g in range(4):
            def em_tr(te, g=g):
                return te.transpose(psT[:], qh32[:, g * 128:(g + 1) * 128],
                                    ident_bf[:])
            p.op("tensor", em_tr, reads=[qh32_b, ident_bf_bb], writes=[psT_b],
                 tag=f"qtr{t}_{g}")
            p.op("vector", lambda v, g=g, t=t: v.tensor_copy(
                q_fm[:, g * T + t * 128: g * T + (t + 1) * 128], psT[:]),
                reads=[psT_b], writes=[qbufs[g]], tag=f"qte{t}_{g}")
    p.barrier()
    es_xq.close()

    es_xk = ExitStack()
    TS = 1024
    kv_sb, _ = ctx.sbuf("kv_sb", [128, TK], BF16, es=es_xk)
    kvbufs = [Buf(f"kv{t}") for t in range(8)]
    xk_bf, _ = ctx.sbuf("xk_bf", [128, DCH * TS], BF16, es=es_xk)
    xkbufs = [Buf(f"xk{c}") for c in range(DCH)]
    wkv_all, _wkv = ctx.sbuf("wkv_all", [128, DCH * 128], BF16, es=es_xk)
    wkv_bufs = [Buf(f"wkv{c}") for c in range(DCH)]
    for c in range(DCH):
        p.dma("gpsimd", out=wkv_all[:, c * 128:(c + 1) * 128],
              in_=wkv[c * 128:(c + 1) * 128, :], writes=[wkv_bufs[c]],
              tag=f"lwkv{c}")
    for st in range(4):
        soff = st * TS
        for c in range(DCH):
            p.dma("gpsimd", out=xk_bf[:, c * TS:(c + 1) * TS],
                  in_=xkT[c * 128:(c + 1) * 128, soff:soff + TS],
                  writes=[xkbufs[c]], tag=f"lxk{st}_{c}")
        emit_stats_and_ln(ctx, consts, rs, f"k{st}", xk_bf, xkbufs, TS, TS,
                          psR1, psR1_b, psR2, psR2_b, x2, x2_b, negm_s,
                          negm_s_b, rstd_s, rstd_s_b, nmr, nmr_b, bc, bc_b,
                          psLN)
        for jr in range(0, TS, 512):
            def em_kv(te, jr=jr):
                last = None
                for c in range(DCH):
                    last = te.matmul(psG[0][0][:],
                                     wkv_all[:, c * 128:(c + 1) * 128],
                                     xk_bf[:, c * TS + jr: c * TS + jr + 512],
                                     start=(c == 0), stop=(c == DCH - 1))
                return last
            p.op("tensor", em_kv, reads=xkbufs + wkv_bufs, writes=[psG[0][1]],
                 tag=f"kv{st}_{jr}")
            p.op("vector", lambda v, jr=jr, soff=soff: v.tensor_copy(
                kv_sb[:, soff + jr: soff + jr + 512], psG[0][0][:]),
                reads=[psG[0][1]], writes=[kvbufs[(soff + jr) // 512]],
                tag=f"kve{st}_{jr}")
        for tt in range(8):
            jt = st * 8 + tt
            jloc = tt * 128
            kvb = kvbufs[(soff + jloc) // 512]

            def em_kt(te, jloc=jloc, soff=soff):
                return te.matmul(psG[0][0][:],
                                 kv_sb[:, soff + jloc: soff + jloc + 128],
                                 wk_sb[:], start=True, stop=True)
            p.op("tensor", em_kt, reads=[kvb, wk_sb_b], writes=[psG[0][1]],
                 tag=f"kt{jt}")
            for h in range(NH):
                def em_ss(s, h=h):
                    return s.activation(out=qsq[:, 0:32],
                                        in_=psG[0][0][:, h * 32:(h + 1) * 32],
                                        func=AF.Square,
                                        accum_out=hss[:, h:h + 1])
                p.op("scalar", em_ss, reads=[psG[0][1]],
                     writes=[qsq_b, hss_b], tag=f"kss{jt}_{h}")
            p.op("scalar", lambda s: s.activation(out=hrstd[:], in_=hss[:],
                                                  func=AF.Ln, scale=0.03125,
                                                  bias=1e-6),
                 reads=[hss_b], writes=[hrstd_b], tag=f"kln{jt}")
            p.op("scalar", lambda s: s.activation(out=hrstd[:], in_=hrstd[:],
                                                  func=AF.Exp, scale=-0.5),
                 reads=[hrstd_b], writes=[hrstd_b], tag=f"kex{jt}")
            for h in range(NH):
                def em_kn(v, h=h):
                    return v.tensor_single_scalar(
                        qh32[:, h * 32:(h + 1) * 32],
                        psG[0][0][:, h * 32:(h + 1) * 32], hrstd[:, h:h + 1],
                        ALU.mult)
                p.op("vector", em_kn, reads=[psG[0][1], hrstd_b],
                     writes=[qh32_b], tag=f"kn{jt}_{h}")

            def em_kg(v):
                return v.tensor_mul(qh32[:], qh32[:], gkv_bc[:])
            p.op("vector", em_kg, reads=[qh32_b, gkv_bc_b], writes=[qh32_b],
                 tag=f"kg{jt}")
            for g in range(4):
                def em_tr(te, g=g):
                    return te.transpose(psT[:], qh32[:, g * 128:(g + 1) * 128],
                                        ident_bf[:])
                p.op("tensor", em_tr, reads=[qh32_b, ident_bf_bb],
                     writes=[psT_b], tag=f"ktr{jt}_{g}")
                p.op("vector", lambda v, g=g, jt=jt: v.tensor_copy(
                    k_fm[:, g * TK + jt * 128: g * TK + (jt + 1) * 128],
                    psT[:]), reads=[psT_b], writes=[kbufs[g]],
                    tag=f"kte{jt}_{g}")

            def em_vt(te, jloc=jloc, soff=soff):
                return te.matmul(psG[1][0][:],
                                 kv_sb[:, soff + jloc: soff + jloc + 128],
                                 wv_sb[:], start=True, stop=True)
            p.op("tensor", em_vt, reads=[kvb, wv_sb_b], writes=[psG[1][1]],
                 tag=f"vt{jt}")

            def em_ve(v, jt=jt):
                return v.tensor_copy(vre[:, jt, :, 0:32], psG[1][0][:])
            p.op("vector", em_ve, reads=[psG[1][1]], writes=[vbufs[jt]],
                 tag=f"ve{jt}")
    p.barrier()
    es_xk.close()
    es_ps.close()
    psS = [ctx.psum(f"psS{i}", [128, 1024], F32) for i in range(2)]
    psA = [ctx.psum(f"psA{i}", [33, 1024], F32) for i in range(2)]
    o_sb, _ = ctx.sbuf("o_sb", [128, 4 * 1024], BF16)
    obufs = [Buf(f"o{g}") for g in range(4)]
    wo_sb, _ = ctx.sbuf("wo_sb", [128, 4 * 1024], BF16)
    wo_bufs = [Buf(f"wo{c}") for c in range(4)]
    for c in range(4):
        p.dma("gpsimd", out=wo_sb[:, c * 1024:(c + 1) * 1024],
              in_=wosel[c * 128:(c + 1) * 128, :], writes=[wo_bufs[c]],
              tag=f"lwo{c}")

    NIR = 2
    IW = T // NIR
    tok = None
    for ir in range(NIR):
        i0 = ir * IW
        for hp in range(8):
            for jt in range(32):
                mbi = jt % 2
                p.dma("sync", out=mbuf[:, mbi * 1024:(mbi + 1) * 1024],
                      in_=vmask[jt * 128:(jt + 1) * 128, i0:i0 + IW],
                      writes=[mbufs[mbi]], tag=f"ml{ir}_{hp}_{jt}")

                def em_s(te, hp=hp, jt=jt, i0=i0):
                    last = None
                    for h2 in range(2):
                        h = hp * 2 + h2
                        g, r = h // 4, (h % 4) * 32
                        lhs = k_fm[r:r + 32,
                                   g * TK + jt * 128: g * TK + (jt + 1) * 128]
                        for j in range(2):
                            last = te.matmul(
                                psS[h2][0][:, j * 512:(j + 1) * 512],
                                lhs,
                                q_fm[r:r + 32,
                                     g * T + i0 + j * 512:
                                     g * T + i0 + (j + 1) * 512],
                                start=True, stop=True,
                                tile_position=(r, 0))
                    return last
                p.op("tensor", em_s, reads=kbufs + qbufs,
                     writes=[psS[0][1], psS[1][1]], tag=f"s{ir}_{hp}_{jt}")
                for h2 in range(2):
                    ebi = h2

                    def em_e(s, h2=h2, ebi=ebi):
                        return s.activation(
                            out=expt[:, ebi * 1024: ebi * 1024 + 1024],
                            in_=psS[h2][0][:], func=AF.Exp, scale=0.125,
                            bias=negc[:])
                    p.op("scalar", em_e, reads=[psS[h2][1], negc_b],
                         writes=[expt_bufs[ebi]], tag=f"e{ir}_{hp}_{jt}_{h2}")

                    def em_ml(v, ebi=ebi, mbi=mbi):
                        return v.tensor_mul(
                            expt[:, ebi * 1024: ebi * 1024 + 1024],
                            expt[:, ebi * 1024: ebi * 1024 + 1024],
                            mbuf[:, mbi * 1024:(mbi + 1) * 1024])
                    p.op("vector", em_ml, reads=[expt_bufs[ebi], mbufs[mbi]],
                         writes=[expt_bufs[ebi]], tag=f"m{ir}_{hp}_{jt}_{h2}")

                    def em_av(te, hp=hp, jt=jt, h2=h2, ebi=ebi):
                        h = hp * 2 + h2
                        last = None
                        for j in range(2):
                            last = te.matmul(
                                psA[h2][0][:, j * 512:(j + 1) * 512],
                                vre3[:, jt * 16 + h, :],
                                expt[:, ebi * 1024 + j * 512:
                                     ebi * 1024 + (j + 1) * 512],
                                start=(jt == 0), stop=(jt == 31))
                        return last
                    p.op("tensor", em_av, reads=[vbufs[jt], expt_bufs[ebi]],
                         writes=[psA[h2][1]], tag=f"av{ir}_{hp}_{jt}_{h2}")
            for h2 in range(2):
                h = hp * 2 + h2
                g, r = h // 4, (h % 4) * 32
                p.op("vector", lambda v, h2=h2: v.reciprocal(
                    nmr[0:1, 0:1024], psA[h2][0][32:33, :]),
                    reads=[psA[h2][1]], writes=[nmr_b],
                    tag=f"rd{ir}_{hp}_{h2}")
                p.op("vector", lambda v: v.tensor_copy(rdbf[0:1, 0:1024],
                                                       nmr[0:1, 0:1024]),
                     reads=[nmr_b], writes=[rdbf_b], tag=f"rdb{ir}_{hp}_{h2}")

                def em_rb(te):
                    last = None
                    for j in range(2):
                        last = te.matmul(
                            psS[0][0][0:32, j * 512:(j + 1) * 512],
                            ones512[0:1, 0:32],
                            rdbf[0:1, j * 512:(j + 1) * 512],
                            start=True, stop=True)
                    return last
                p.op("tensor", em_rb, reads=[rdbf_b, ones512_bb],
                     writes=[psS[0][1]], tag=f"rbb{ir}_{hp}_{h2}")
                p.op("vector", lambda v: v.tensor_copy(rdbc[0:32, 0:1024],
                                                       psS[0][0][0:32, :]),
                     reads=[psS[0][1]], writes=[rdbc_b],
                     tag=f"rbe{ir}_{hp}_{h2}")

                def em_oe(v, g=g, r=r, h2=h2):
                    return v.tensor_mul(
                        o_sb[r:r + 32, g * 1024:(g + 1) * 1024],
                        psA[h2][0][0:32, :], rdbc[0:32, 0:1024])
                p.op("vector", em_oe, reads=[psA[h2][1], rdbc_b],
                     writes=[obufs[g]], tag=f"oe{ir}_{hp}_{h2}")
        for j2 in range(0, IW, 512):
            for dc in range(DCH):
                pi = psS[dc % 2]

                def em_w(te, dc=dc, j2=j2, pi=pi):
                    last = None
                    for ec in range(4):
                        last = te.matmul(pi[0][:, 0:512],
                                         wo_sb[:, ec * 1024 + dc * 128:
                                               ec * 1024 + (dc + 1) * 128],
                                         o_sb[:, ec * 1024 + j2:
                                              ec * 1024 + j2 + 512],
                                         start=(ec == 0), stop=(ec == 3))
                    return last
                p.op("tensor", em_w, reads=obufs + wo_bufs, writes=[pi[1]],
                     tag=f"w{ir}_{j2}_{dc}")
                p.dma("sync", out=xres[:],
                      in_=xqT[dc * 128:(dc + 1) * 128,
                              i0 + j2: i0 + j2 + 512],
                      writes=[xres_b], tag=f"wx{ir}_{j2}_{dc}")
                yi = dc % 2

                def em_y(v, pi=pi, yi=yi):
                    return v.tensor_add(ysb[:, yi * 512:(yi + 1) * 512],
                                        pi[0][:, 0:512], xres[:])
                p.op("vector", em_y, reads=[pi[1], xres_b],
                     writes=[ysb_bufs[yi]], tag=f"wy{ir}_{j2}_{dc}")
                tok = p.dma("sync",
                            out=yT[dc * 128:(dc + 1) * 128,
                                   i0 + j2: i0 + j2 + 512],
                            in_=ysb[:, yi * 512:(yi + 1) * 512],
                            reads=[ysb_bufs[yi]], tag=f"wst{ir}_{j2}_{dc}")
    p.finish_outputs([tok])
    p.emit(es)
    es.close()
    return nc


def host_mla_inmaps(X, inputs):
    consts = host_consts()
    gq = np.asarray(inputs["L1_attn_a_q_g"], np.float32)
    gk = np.asarray(inputs["L1_attn_a_k_g"], np.float32)
    wo = np.asarray(inputs["L1_attn_a_Wo"], np.float32)
    wo_sel = _ct(np.concatenate([wo[h * 64: h * 64 + 32, :]
                                 for h in range(16)], axis=0))
    shared = dict(wq=_ct(np.asarray(inputs["L1_attn_a_Wq"], np.float32)),
                  wkv=_ct(np.asarray(inputs["L1_attn_a_Wkv"], np.float32)),
                  wk=_ct(np.asarray(inputs["L1_attn_a_Wk"], np.float32)),
                  wv=_ct(np.asarray(inputs["L1_attn_a_Wv"], np.float32)),
                  wosel=wo_sel,
                  gqt=_ct(np.tile(gq[None, :32], (1, 16))),
                  gkt=_ct(np.tile(gk[None, :], (1, 16))),
                  fconsts=host_fconsts(FCONST_VALUES),
                  **{k: consts[k] for k in ["ones_col_bf", "ones_row_f",
                                            "ones_row_bf", "ones_row512",
                                            "ident_bf"]})
    jg = np.arange(L)[:, None]
    ims = []
    for c in range(NCORES):
        s = c % 2
        ig = np.arange(TLOC)[None, :] + s * TLOC
        vmask = _ct((jg <= ig).astype(BF))
        ims.append(dict(xqT=X[c],
                        xkT=_ct(np.concatenate([X[(c // 2) * 2],
                                                X[(c // 2) * 2 + 1]], axis=1)),
                        vmask=vmask, **shared))
    return ims


def run_mla(X, inputs):
    if "mla" not in _CACHE:
        _CACHE["mla"] = build_mla()
    im = host_mla_inmaps(X, inputs)
    res = run_bass_kernel_spmd(_CACHE["mla"], im, core_ids=list(range(NCORES)))
    return [res.results[c]["yT"] for c in range(NCORES)]



def build_retnet():
    nc = bass.Bass("TRN2", target_bir_lowering=False, debug=False)
    T = TLOC
    NHH = 8

    xqT = nc.dram_tensor("xqT", [D, T], F32, kind="ExternalInput").ap()
    xpT = nc.dram_tensor("xpT", [D, T], F32, kind="ExternalInput").ap()
    wqr = nc.dram_tensor("wqr", [D, D], F32, kind="ExternalInput").ap()
    wkr = nc.dram_tensor("wkr", [D, D], F32, kind="ExternalInput").ap()
    wvr = nc.dram_tensor("wvr", [D, D], F32, kind="ExternalInput").ap()
    wor = nc.dram_tensor("wor", [D, D], F32, kind="ExternalInput").ap()
    dmat = nc.dram_tensor("dmat", [128, NHH * 128], F32,
                          kind="ExternalInput").ap()
    dqmat = nc.dram_tensor("dqmat", [128, NHH * 128], F32,
                           kind="ExternalInput").ap()
    dkcol = nc.dram_tensor("dkcol", [128, NHH], F32, kind="ExternalInput").ap()
    dscol = nc.dram_tensor("dscol", [128, NHH], F32, kind="ExternalInput").ap()
    yT = nc.dram_tensor("yT", [D, T], F32, kind="ExternalOutput").ap()

    es = ExitStack()
    p = Prog(nc)
    ctx = Ctx(nc, es, p)
    consts = load_consts(ctx, ["ones_col_bf", "ones_row_f"])
    load_fconsts(ctx, FCONST_VALUES)
    ones_bf, ones_bf_bb = consts["ones_col_bf"]
    ones_row, ones_row_bb = consts["ones_row_f"]

    negm_s, negm_s_b = ctx.sbuf("negm_s", [1, T], BF16)
    rstd_s, rstd_s_b = ctx.sbuf("rstd_s", [1, T], F32)
    nmr, nmr_b = ctx.sbuf("nmr", [1, 1024], F32)
    bc, bc_b = ctx.sbuf("bc", [128, 1024], F32)
    x2, x2_b = ctx.sbuf("x2", [128, 1024], BF16)
    rs = make_row_scratch(ctx)
    dm_sb, dm_sb_b = ctx.sbuf("dm_sb", [128, NHH * 128], BF16)
    dq_sb, dq_sb_b = ctx.sbuf("dq_sb", [128, NHH * 128], BF16)
    dk_sb, dk_sb_b = ctx.sbuf("dk_sb", [128, NHH], F32)
    ds_sb, ds_sb_b = ctx.sbuf("ds_sb", [128, NHH], F32)
    S_f, S_f_b = ctx.sbuf("S_f", [128, 128], F32)
    S_bf, S_bf_b = ctx.sbuf("S_bf", [128, 128], BF16)
    kt_bf, kt_bf_b = ctx.sbuf("kt_bf", [128, 128], BF16)
    kf_bf, kf_bf_b = ctx.sbuf("kf_bf", [128, 128], BF16)
    vt_bf, vt_bf_b = ctx.sbuf("vt_bf", [128, 128], BF16)
    q_bf, q_bf_b = ctx.sbuf("q_bf", [128, 128], BF16)
    qd_bf, qd_bf_b = ctx.sbuf("qd_bf", [128, 128], BF16)
    at_bf, at_bf_b = ctx.sbuf("at_bf", [128, 128], BF16)
    xres, xres_b = ctx.sbuf("xres", [128, 512], F32)
    ysb, _ys = ctx.sbuf("ysb", [128, 2 * 512], F32)
    ysb_bufs = [Buf("ysb0"), Buf("ysb1")]

    xq_bf, _ = ctx.sbuf("xq_bf", [128, DCH * T], BF16)
    xqbufs = [Buf(f"xq{c}") for c in range(DCH)]
    xp_bf, _ = ctx.sbuf("xp_bf", [128, DCH * T], BF16)
    xpbufs = [Buf(f"xp{c}") for c in range(DCH)]
    o_sb, _ = ctx.sbuf("o_sb", [128, NHH * T], BF16)
    obufs = [Buf(f"o{h}") for h in range(NHH)]
    wq_all, _w1 = ctx.sbuf("wq_all", [128, DCH * D], BF16)
    wq_bufs = [Buf(f"wqa{c}") for c in range(DCH)]
    wk_all, _w2 = ctx.sbuf("wk_all", [128, DCH * D], BF16)
    wk_bufs = [Buf(f"wka{c}") for c in range(DCH)]
    wv_all, _w3 = ctx.sbuf("wv_all", [128, DCH * D], BF16)
    wv_bufs = [Buf(f"wva{c}") for c in range(DCH)]
    wo_all, _w4 = ctx.sbuf("wo_all", [128, DCH * D], BF16)
    wo_bufs = [Buf(f"woa{c}") for c in range(DCH)]

    es_ps = ExitStack()
    psG = [ctx.psum(f"psG{i}", [128, 512], F32, es=es_ps) for i in range(2)]
    psR1, psR1_b = ctx.psum("psR1", [1, 1024], F32, es=es_ps)
    psR2, psR2_b = ctx.psum("psR2", [1, 1024], F32, es=es_ps)
    psLN = [(psG[0][0][:], psG[0][1]), (psG[1][0][:], psG[1][1])]

    for c in range(DCH):
        p.dma("gpsimd", out=xq_bf[:, c * T:(c + 1) * T],
              in_=xqT[c * 128:(c + 1) * 128, :], writes=[xqbufs[c]],
              tag=f"lxq{c}")
        p.dma("gpsimd", out=xp_bf[:, c * T:(c + 1) * T],
              in_=xpT[c * 128:(c + 1) * 128, :], writes=[xpbufs[c]],
              tag=f"lxp{c}")
        p.dma("gpsimd", out=wq_all[:, c * D:(c + 1) * D],
              in_=wqr[c * 128:(c + 1) * 128, :], writes=[wq_bufs[c]],
              tag=f"lwq{c}")
        p.dma("gpsimd", out=wk_all[:, c * D:(c + 1) * D],
              in_=wkr[c * 128:(c + 1) * 128, :], writes=[wk_bufs[c]],
              tag=f"lwk{c}")
        p.dma("gpsimd", out=wv_all[:, c * D:(c + 1) * D],
              in_=wvr[c * 128:(c + 1) * 128, :], writes=[wv_bufs[c]],
              tag=f"lwv{c}")
        p.dma("gpsimd", out=wo_all[:, c * D:(c + 1) * D],
              in_=wor[c * 128:(c + 1) * 128, :], writes=[wo_bufs[c]],
              tag=f"lwo{c}")
    p.dma("gpsimd", out=dm_sb[:], in_=dmat, writes=[dm_sb_b], tag="ldm")
    p.dma("gpsimd", out=dq_sb[:], in_=dqmat, writes=[dq_sb_b], tag="ldq")
    p.dma("sync", out=dk_sb[:], in_=dkcol, writes=[dk_sb_b], tag="ldk")
    p.dma("sync", out=ds_sb[:], in_=dscol, writes=[ds_sb_b], tag="lds")

    emit_stats_and_ln(ctx, consts, rs, "q", xq_bf, xqbufs, T, T, psR1, psR1_b,
                      psR2, psR2_b, x2, x2_b, negm_s, negm_s_b, rstd_s,
                      rstd_s_b, nmr, nmr_b, bc, bc_b, psLN)
    emit_stats_and_ln(ctx, consts, rs, "p", xp_bf, xpbufs, T, T, psR1, psR1_b,
                      psR2, psR2_b, x2, x2_b, negm_s, negm_s_b, rstd_s,
                      rstd_s_b, nmr, nmr_b, bc, bc_b, psLN)

    p.barrier()
    es_ps.close()
    psP = [ctx.psum(f"psP{i}", [128, 128], F32) for i in range(4)]

    def proj(xbufs_, x_sb, w_all, w_bufs, hcol, tile, ps, role_lhsT_w,
             tag):
        def em(te):
            last = None
            for c in range(DCH):
                if role_lhsT_w:
                    last = te.matmul(ps[0][:],
                                     w_all[:, c * D + hcol: c * D + hcol + 128],
                                     x_sb[:, c * T + tile * 128:
                                          c * T + (tile + 1) * 128],
                                     start=(c == 0), stop=(c == DCH - 1))
                else:
                    last = te.matmul(ps[0][:],
                                     x_sb[:, c * T + tile * 128:
                                          c * T + (tile + 1) * 128],
                                     w_all[:, c * D + hcol: c * D + hcol + 128],
                                     start=(c == 0), stop=(c == DCH - 1))
            return last
        p.op("tensor", em, reads=xbufs_ + w_bufs, writes=[ps[1]], tag=tag)

    for h in range(NHH):
        hcol = h * 128
        p.op("vector", lambda v: v.memset(S_f[:], 0.0), writes=[S_f_b],
             tag=f"sz{h}")
        p.op("vector", lambda v: v.tensor_copy(S_bf[:], S_f[:]),
             reads=[S_f_b], writes=[S_bf_b], tag=f"szb{h}")
        for ph in range(2):
            x_sb = xp_bf if ph == 0 else xq_bf
            xbufs_ = xpbufs if ph == 0 else xqbufs
            for tile in range(16):
                proj(xbufs_, x_sb, wk_all, wk_bufs, hcol, tile, psP[0], False,
                     f"kt{h}_{ph}_{tile}")

                def em_kd(v, h=h):
                    return v.tensor_single_scalar(kt_bf[:], psP[0][0][:],
                                                  dk_sb[:, h:h + 1], ALU.mult)
                p.op("vector", em_kd, reads=[psP[0][1], dk_sb_b],
                     writes=[kt_bf_b], tag=f"kd{h}_{ph}_{tile}")
                proj(xbufs_, x_sb, wv_all, wv_bufs, hcol, tile, psP[1], False,
                     f"vt{h}_{ph}_{tile}")
                p.op("vector", lambda v: v.tensor_copy(vt_bf[:], psP[1][0][:]),
                     reads=[psP[1][1]], writes=[vt_bf_b],
                     tag=f"ve{h}_{ph}_{tile}")
                if ph == 1:
                    proj(xbufs_, x_sb, wk_all, wk_bufs, hcol, tile, psP[0],
                         True, f"kf{h}_{tile}")
                    p.op("vector", lambda v: v.tensor_copy(kf_bf[:],
                                                           psP[0][0][:]),
                         reads=[psP[0][1]], writes=[kf_bf_b],
                         tag=f"kfe{h}_{tile}")
                    proj(xbufs_, x_sb, wq_all, wq_bufs, hcol, tile, psP[1],
                         True, f"qf{h}_{tile}")
                    p.op("vector", lambda v: v.tensor_copy(q_bf[:],
                                                           psP[1][0][:]),
                         reads=[psP[1][1]], writes=[q_bf_b],
                         tag=f"qfe{h}_{tile}")

                    def em_qd(v, h=h):
                        return v.tensor_mul(qd_bf[:], q_bf[:],
                                            dq_sb[:, h * 128:(h + 1) * 128])
                    p.op("vector", em_qd, reads=[q_bf_b, dq_sb_b],
                         writes=[qd_bf_b], tag=f"qd{h}_{tile}")
                    def em_at(te):
                        return te.matmul(psP[0][0][:], kf_bf[:], q_bf[:],
                                         start=True, stop=True)
                    p.op("tensor", em_at, reads=[kf_bf_b, q_bf_b],
                         writes=[psP[0][1]], tag=f"at{h}_{tile}")

                    def em_ad(v, h=h):
                        return v.tensor_mul(at_bf[:], psP[0][0][:],
                                            dm_sb[:, h * 128:(h + 1) * 128])
                    p.op("vector", em_ad, reads=[psP[0][1], dm_sb_b],
                         writes=[at_bf_b], tag=f"ad{h}_{tile}")
                    def em_o(te):
                        te.matmul(psP[2][0][:], vt_bf[:], at_bf[:],
                                  start=True, stop=False)
                        return te.matmul(psP[2][0][:], S_bf[:], qd_bf[:],
                                         start=False, stop=True)
                    p.op("tensor", em_o, reads=[vt_bf_b, at_bf_b, S_bf_b,
                                                qd_bf_b],
                         writes=[psP[2][1]], tag=f"o{h}_{tile}")
                    p.op("vector", lambda v, h=h, tile=tile: v.tensor_copy(
                        o_sb[:, h * T + tile * 128: h * T + (tile + 1) * 128],
                        psP[2][0][:]), reads=[psP[2][1]], writes=[obufs[h]],
                        tag=f"oe{h}_{tile}")
                def em_u(te):
                    return te.matmul(psP[3][0][:], kt_bf[:], vt_bf[:],
                                     start=True, stop=True)
                p.op("tensor", em_u, reads=[kt_bf_b, vt_bf_b],
                     writes=[psP[3][1]], tag=f"u{h}_{ph}_{tile}")

                def em_su(v, h=h):
                    return v.scalar_tensor_tensor(S_f[:], S_f[:],
                                                  ds_sb[:, h:h + 1],
                                                  psP[3][0][:], ALU.mult,
                                                  ALU.add)
                p.op("vector", em_su, reads=[S_f_b, ds_sb_b, psP[3][1]],
                     writes=[S_f_b], tag=f"su{h}_{ph}_{tile}")
                p.op("vector", lambda v: v.tensor_copy(S_bf[:], S_f[:]),
                     reads=[S_f_b], writes=[S_bf_b], tag=f"sb{h}_{ph}_{tile}")

    p.barrier()
    psW = [ctx.psum(f"psW{i}", [128, 512], F32) for i in range(2)]
    tok = None
    for j2 in range(0, T, 512):
        for dc in range(DCH):
            pi = psW[dc % 2]

            def em_w(te, dc=dc, j2=j2, pi=pi):
                last = None
                for h in range(NHH):
                    last = te.matmul(pi[0][:],
                                     wo_all[:, h * D + dc * 128:
                                            h * D + dc * 128 + 128],
                                     o_sb[:, h * T + j2: h * T + j2 + 512],
                                     start=(h == 0), stop=(h == NHH - 1))
                return last
            p.op("tensor", em_w, reads=obufs + wo_bufs, writes=[pi[1]],
                 tag=f"w{j2}_{dc}")
            p.dma("sync", out=xres[:],
                  in_=xqT[dc * 128:(dc + 1) * 128, j2:j2 + 512],
                  writes=[xres_b], tag=f"wx{j2}_{dc}")
            yi = dc % 2

            def em_y(v, pi=pi, yi=yi):
                return v.tensor_add(ysb[:, yi * 512:(yi + 1) * 512],
                                    pi[0][:], xres[:])
            p.op("vector", em_y, reads=[pi[1], xres_b], writes=[ysb_bufs[yi]],
                 tag=f"wy{j2}_{dc}")
            tok = p.dma("sync", out=yT[dc * 128:(dc + 1) * 128, j2:j2 + 512],
                        in_=ysb[:, yi * 512:(yi + 1) * 512],
                        reads=[ysb_bufs[yi]], tag=f"wst{j2}_{dc}")
    p.finish_outputs([tok])
    p.emit(es)
    es.close()
    return nc


def host_retnet_inmaps(X, inputs):
    consts = host_consts()
    C = 128
    gammas = 1.0 - 2.0 ** (-5.0 - np.arange(8, dtype=np.float64))
    s = 128.0 ** -0.5
    idx = np.arange(C)
    dmat = np.zeros((8, C, C), np.float32)
    dqmat = np.zeros((8, C, C), np.float32)
    dkcol = np.zeros((C, 8), np.float32)
    dscol = np.zeros((C, 8), np.float32)
    for h, g in enumerate(gammas):
        diff = idx[None, :] - idx[:, None]
        dmat[h] = np.where(diff >= 0, g ** diff, 0.0) * s
        dqmat[h] = np.tile((g ** (idx + 1.0))[None, :] * s, (C, 1))
        dkcol[:, h] = g ** (C - 1.0 - idx)
        dscol[:, h] = g ** C
    shared = dict(wqr=_ct(np.asarray(inputs["L0_attn_a_Wq"], np.float32)),
                  wkr=_ct(np.asarray(inputs["L0_attn_a_Wk"], np.float32)),
                  wvr=_ct(np.asarray(inputs["L0_attn_a_Wv"], np.float32)),
                  wor=_ct(np.asarray(inputs["L0_attn_a_Wo"], np.float32)),
                  dmat=_ct(dmat.transpose(1, 0, 2).reshape(C, 8 * C)),
                  dqmat=_ct(dqmat.transpose(1, 0, 2).reshape(C, 8 * C)),
                  dkcol=dkcol, dscol=dscol,
                  fconsts=host_fconsts(FCONST_VALUES),
                  **{k: consts[k] for k in ["ones_col_bf", "ones_row_f"]})
    zeros = np.zeros((D, TLOC), np.float32)
    ims = []
    for c in range(NCORES):
        s_ = c % 2
        xp = zeros if s_ == 0 else X[(c // 2) * 2]
        ims.append(dict(xqT=X[c], xpT=_ct(xp), **shared))
    return ims


def run_retnet(X, inputs):
    if "retnet" not in _CACHE:
        _CACHE["retnet"] = build_retnet()
    im = host_retnet_inmaps(X, inputs)
    res = run_bass_kernel_spmd(_CACHE["retnet"], im,
                               core_ids=list(range(NCORES)))
    return [res.results[c]["yT"] for c in range(NCORES)]



def build_mamba_pre():
    nc = bass.Bass("TRN2", target_bir_lowering=False, debug=False)
    T = TLOC
    TP = 128
    T7 = T + TP
    E = 2048
    ECH = 16

    xqT = nc.dram_tensor("xqT", [D, T], F32, kind="ExternalInput").ap()
    xtT = nc.dram_tensor("xtT", [D, TP], F32, kind="ExternalInput").ap()
    wint = nc.dram_tensor("wint", [32 * 128, 8 * 128], F32,
                          kind="ExternalInput").ap()
    binrow = nc.dram_tensor("binrow", [1, 4096], F32,
                            kind="ExternalInput").ap()
    convw = nc.dram_tensor("convw", [128, 64], F32, kind="ExternalInput").ap()
    cbblk = nc.dram_tensor("cbblk", [128, ECH], F32,
                           kind="ExternalInput").ap()
    xpw = nc.dram_tensor("xpw", [E, 160], F32, kind="ExternalInput").ap()
    dtw = nc.dram_tensor("dtw", [128, E], F32, kind="ExternalInput").ap()
    dtbrowd = nc.dram_tensor("dtbrow", [1, E], F32,
                             kind="ExternalInput").ap()
    bmaskd = nc.dram_tensor("bmask", [1, 2176], F32,
                            kind="ExternalInput").ap()
    xcs = nc.dram_tensor("xcs", [E, T], BF16, kind="ExternalOutput").ap()
    xrs = nc.dram_tensor("xrs", [E, T], BF16, kind="ExternalOutput").ap()
    dts = nc.dram_tensor("dts", [E, T], BF16, kind="ExternalOutput").ap()
    bcs = nc.dram_tensor("bcs", [32, T], BF16, kind="ExternalOutput").ap()

    es = ExitStack()
    p = Prog(nc)
    ctx = Ctx(nc, es, p)
    consts = load_consts(ctx, ["ones_col_bf", "ones_row_f", "ones_row512"])
    load_fconsts(ctx, FCONST_VALUES)
    ones_bf, ones_bf_bb = consts["ones_col_bf"]
    ones_row, ones_row_bb = consts["ones_row_f"]
    ones512, ones512_bb = consts["ones_row512"]

    negm_s, negm_s_b = ctx.sbuf("negm_s", [1, T7], BF16)
    rstd_s, rstd_s_b = ctx.sbuf("rstd_s", [1, T7], F32)
    nmr, nmr_b = ctx.sbuf("nmr", [1, 1024], F32)
    bc, bc_b = ctx.sbuf("bc", [128, 1024], F32)
    x2, x2_b = ctx.sbuf("x2", [128, 1024], BF16)
    rs = make_row_scratch(ctx)
    binsb, binsb_b = ctx.sbuf("binsb", [1, 4096], BF16)
    cw_sb, cw_sb_b = ctx.sbuf("cw_sb", [128, 64], F32)
    cb_sb, cb_sb_b = ctx.sbuf("cb_sb", [128, ECH], F32)
    dtbrow, dtbrow_b = ctx.sbuf("dtbrow_sb", [1, E], BF16)
    bmask, bmask_b = ctx.sbuf("bmask_sb", [1, 2176], BF16)
    zc_sb, zc_sb_b = ctx.sbuf("zc_sb", [128, T7], BF16)
    cvt, cvt_b = ctx.sbuf("cvt", [128, T], F32)
    winb, _wi = ctx.sbuf("winb", [128, 2 * 1024], BF16)
    winbufs = [Buf("wi0"), Buf("wi1")]
    xpw_all, _xp = ctx.sbuf("xpw_all", [128, ECH * 160], BF16)
    xpw_bufs = [Buf(f"xpw{c}") for c in range(ECH)]
    dtw_all, _dt = ctx.sbuf("dtw_all", [128, ECH * 128], BF16)
    dtw_bufs = [Buf(f"dtw{c}") for c in range(ECH)]
    projA, projA_b = ctx.sbuf("projA", [128, T], BF16)
    sp0, sp0_b = ctx.sbuf("sp0", [128, 512], F32)
    sp1, sp1_b = ctx.sbuf("sp1", [128, 512], F32)
    sp2, sp2_b = ctx.sbuf("sp2", [128, 512], F32)
    ev1, ev1_b = ctx.sbuf("ev1", [128, 2 * 512], BF16)
    ev_bufs = [Buf("ev0"), Buf("ev1")]

    x_bf, _ = ctx.sbuf("x_bf", [128, DCH * T7], BF16)
    xbufs = [Buf(f"x{c}") for c in range(DCH)]
    xc_all, _xc = ctx.sbuf("xc_all", [128, ECH * T], BF16)
    xcbufs = [Buf(f"xc{c}") for c in range(ECH)]

    es_ps = ExitStack()
    psG = [ctx.psum(f"psG{i}", [128, 512], F32, es=es_ps) for i in range(2)]
    psR1, psR1_b = ctx.psum("psR1", [1, 1024], F32, es=es_ps)
    psR2, psR2_b = ctx.psum("psR2", [1, 1024], F32, es=es_ps)
    psLN = [(psG[0][0][:], psG[0][1]), (psG[1][0][:], psG[1][1])]

    for c in range(DCH):
        p.dma("gpsimd", out=x_bf[:, c * T7: c * T7 + TP],
              in_=xtT[c * 128:(c + 1) * 128, :], writes=[xbufs[c]],
              tag=f"lxt{c}")
        p.dma("gpsimd", out=x_bf[:, c * T7 + TP: c * T7 + T7],
              in_=xqT[c * 128:(c + 1) * 128, :], writes=[xbufs[c]],
              tag=f"lxq{c}")
    p.dma("gpsimd", out=binsb[:], in_=binrow, writes=[binsb_b], tag="lbin")
    p.dma("sync", out=cw_sb[:], in_=convw, writes=[cw_sb_b], tag="lcw")
    p.dma("sync", out=cb_sb[:], in_=cbblk, writes=[cb_sb_b], tag="lcb")
    p.dma("gpsimd", out=dtbrow[:], in_=dtbrowd, writes=[dtbrow_b], tag="ldtb")
    p.dma("gpsimd", out=bmask[:], in_=bmaskd, writes=[bmask_b], tag="lbm")
    for c in range(ECH):
        p.dma("gpsimd", out=xpw_all[:, c * 160:(c + 1) * 160],
              in_=xpw[c * 128:(c + 1) * 128, :], writes=[xpw_bufs[c]],
              tag=f"lxpw{c}")
        p.dma("gpsimd", out=dtw_all[:, c * 128:(c + 1) * 128],
              in_=dtw[:, c * 128:(c + 1) * 128], writes=[dtw_bufs[c]],
              tag=f"ldtw{c}")

    emit_stats_and_ln(ctx, consts, rs, "m", x_bf, xbufs, T7, T7, psR1, psR1_b,
                      psR2, psR2_b, x2, x2_b, negm_s, negm_s_b, rstd_s,
                      rstd_s_b, nmr, nmr_b, bc, bc_b, psLN)

    ranges7 = [(j, min(512, T7 - j)) for j in range(0, T7, 512)]
    rangesT = [(j, 512) for j in range(0, T, 512)]

    for cc in range(ECH if S7A_TRUNC >= 2 else 0):
        bi = cc % 2
        p.dma("gpsimd", out=winb[:, bi * 1024:(bi + 1) * 1024],
              in_=wint[cc * 128:(cc + 1) * 128, :], writes=[winbufs[bi]],
              tag=f"lwin{cc}")
        for (j0, w) in ranges7:
            def em_z(te, cc=cc, bi=bi, j0=j0, w=w):
                for c in range(DCH):
                    te.matmul(psG[0][0][:, 0:w],
                              winb[:, bi * 1024 + c * 128:
                                   bi * 1024 + (c + 1) * 128],
                              x_bf[:, c * T7 + j0: c * T7 + j0 + w],
                              start=(c == 0), stop=False)
                return te.matmul(psG[0][0][:, 0:w],
                                 binsb[0:1, cc * 128:(cc + 1) * 128],
                                 bmask[0:1, j0:j0 + w], start=False,
                                 stop=True)
            p.op("tensor", em_z, reads=xbufs + [winbufs[bi], binsb_b,
                                                bmask_b],
                 writes=[psG[0][1]], tag=f"z{cc}_{j0}")
            p.op("vector", lambda v, j0=j0, w=w: v.tensor_copy(
                zc_sb[:, j0:j0 + w], psG[0][0][:, 0:w]),
                reads=[psG[0][1]], writes=[zc_sb_b], tag=f"ze{cc}_{j0}")
        def em_c0(v, cc=cc):
            return v.tensor_single_scalar(cvt[:], zc_sb[:, 125:125 + T],
                                          cw_sb[:, 0 * ECH + cc:
                                                0 * ECH + cc + 1], ALU.mult)
        p.op("vector", em_c0, reads=[zc_sb_b, cw_sb_b], writes=[cvt_b],
             tag=f"c0{cc}")
        for k in range(1, 4):
            def em_ck(v, cc=cc, k=k):
                return v.scalar_tensor_tensor(cvt[:], zc_sb[:, 125 + k:
                                                            125 + k + T],
                                              cw_sb[:, k * ECH + cc:
                                                    k * ECH + cc + 1],
                                              cvt[:], ALU.mult, ALU.add)
            p.op("vector", em_ck, reads=[zc_sb_b, cw_sb_b, cvt_b],
                 writes=[cvt_b], tag=f"c{k}{cc}")

        def em_si(s, cc=cc):
            return s.activation(out=xc_all[:, cc * T:(cc + 1) * T],
                                in_=cvt[:], func=AF.Silu,
                                bias=cb_sb[:, cc:cc + 1])
        p.op("scalar", em_si, reads=[cvt_b, cb_sb_b], writes=[xcbufs[cc]],
             tag=f"si{cc}")
        p.dma("sync", out=xcs[cc * 128:(cc + 1) * 128, :],
              in_=xc_all[:, cc * T:(cc + 1) * T], reads=[xcbufs[cc]],
              tag=f"xco{cc}")

    for (j0, w) in (rangesT if S7A_TRUNC >= 3 else []):
        def em_pa(te, j0=j0):
            last = None
            for c in range(ECH):
                last = te.matmul(psG[0][0][:],
                                 xpw_all[:, c * 160: c * 160 + 128],
                                 xc_all[:, c * T + j0: c * T + j0 + 512],
                                 start=(c == 0), stop=(c == ECH - 1))
            return last
        p.op("tensor", em_pa, reads=xcbufs + xpw_bufs, writes=[psG[0][1]],
             tag=f"pa{j0}")
        p.op("vector", lambda v, j0=j0: v.tensor_copy(
            projA[:, j0:j0 + 512], psG[0][0][:]),
            reads=[psG[0][1]], writes=[projA_b], tag=f"pae{j0}")

        def em_pb(te, j0=j0):
            last = None
            for c in range(ECH):
                last = te.matmul(psG[1][0][0:32, :],
                                 xpw_all[:, c * 160 + 128: c * 160 + 160],
                                 xc_all[:, c * T + j0: c * T + j0 + 512],
                                 start=(c == 0), stop=(c == ECH - 1))
            return last
        p.op("tensor", em_pb, reads=xcbufs + xpw_bufs, writes=[psG[1][1]],
             tag=f"pb{j0}")
        p.op("vector", lambda v: v.tensor_copy(ev1[0:32, 0:512],
                                               psG[1][0][0:32, :]),
             reads=[psG[1][1]], writes=[ev_bufs[0]], tag=f"pbe{j0}")
        p.dma("sync", out=bcs[:, j0:j0 + 512], in_=ev1[0:32, 0:512],
              reads=[ev_bufs[0]], tag=f"pbo{j0}")

    for ec in range(ECH if S7A_TRUNC >= 4 else 0):
        for (j0, w) in rangesT:
            pbi = (j0 // 512) % 2

            def em_d(te, ec=ec, j0=j0):
                te.matmul(psG[0][0][:], dtw_all[:, ec * 128:(ec + 1) * 128],
                          projA[:, j0:j0 + 512], start=True, stop=False)
                return te.matmul(psG[0][0][:],
                                 dtbrow[0:1, ec * 128:(ec + 1) * 128],
                                 ones512[:], start=False, stop=True)
            p.op("tensor", em_d, reads=[projA_b, dtw_bufs[ec], dtbrow_b,
                                        ones512_bb],
                 writes=[psG[0][1]], tag=f"d{ec}_{j0}")
            p.op("vector", lambda v: v.tensor_copy(sp0[:], psG[0][0][:]),
                 reads=[psG[0][1]], writes=[sp0_b], tag=f"dp0{ec}_{j0}")
            p.op("scalar", lambda s: s.activation(out=sp1[:],
                                                  in_=sp0[:],
                                                  func=AF.Abs),
                 reads=[sp0_b], writes=[sp1_b], tag=f"dp1{ec}_{j0}")
            p.op("scalar", lambda s: s.activation(out=sp1[:], in_=sp1[:],
                                                  func=AF.Exp, scale=-1.0),
                 reads=[sp1_b], writes=[sp1_b], tag=f"dp2{ec}_{j0}")
            p.op("scalar", lambda s: s.activation(out=sp1[:], in_=sp1[:],
                                                  func=AF.Ln, bias=1.0),
                 reads=[sp1_b], writes=[sp1_b], tag=f"dp3{ec}_{j0}")
            p.op("vector", lambda v: v.tensor_scalar_max(sp2[:],
                                                         sp0[:], 0.0),
                 reads=[sp0_b], writes=[sp2_b], tag=f"dp4{ec}_{j0}")

            def em_d5(v, pbi=pbi):
                return v.tensor_add(ev1[:, pbi * 512:(pbi + 1) * 512],
                                    sp1[:], sp2[:])
            p.op("vector", em_d5, reads=[sp1_b, sp2_b],
                 writes=[ev_bufs[pbi]], tag=f"dp5{ec}_{j0}")
            p.dma("sync", out=dts[ec * 128:(ec + 1) * 128, j0:j0 + 512],
                  in_=ev1[:, pbi * 512:(pbi + 1) * 512],
                  reads=[ev_bufs[pbi]], tag=f"dout{ec}_{j0}")

    tok = None
    for cc in range(ECH if S7A_TRUNC >= 5 else 0):
        bi = cc % 2
        p.dma("gpsimd", out=winb[:, bi * 1024:(bi + 1) * 1024],
              in_=wint[(16 + cc) * 128:(17 + cc) * 128, :],
              writes=[winbufs[bi]], tag=f"lwr{cc}")
        for (j0, w) in rangesT:
            pbi = (j0 // 512) % 2

            def em_zr(te, cc=cc, bi=bi, j0=j0):
                for c in range(DCH):
                    te.matmul(psG[0][0][:],
                              winb[:, bi * 1024 + c * 128:
                                   bi * 1024 + (c + 1) * 128],
                              x_bf[:, c * T7 + TP + j0: c * T7 + TP + j0 + 512],
                              start=(c == 0), stop=False)
                return te.matmul(psG[0][0][:],
                                 binsb[0:1, 2048 + cc * 128:
                                       2048 + (cc + 1) * 128],
                                 ones512[:], start=False, stop=True)
            p.op("tensor", em_zr, reads=xbufs + [winbufs[bi], binsb_b,
                                                 ones512_bb],
                 writes=[psG[0][1]], tag=f"zr{cc}_{j0}")

            def em_sr(s, pbi=pbi):
                return s.activation(out=ev1[:, pbi * 512:(pbi + 1) * 512],
                                    in_=psG[0][0][:], func=AF.Silu)
            p.op("scalar", em_sr, reads=[psG[0][1]], writes=[ev_bufs[pbi]],
                 tag=f"sr{cc}_{j0}")
            tok = p.dma("sync", out=xrs[cc * 128:(cc + 1) * 128, j0:j0 + 512],
                        in_=ev1[:, pbi * 512:(pbi + 1) * 512],
                        reads=[ev_bufs[pbi]], tag=f"xro{cc}_{j0}")

    p.finish_outputs([tok] if tok is not None else ())
    p.emit(es)
    es.close()
    return nc


def host_mamba_pre_inmaps(X, inputs):
    consts = host_consts()
    win = np.asarray(inputs["L1_attn_b_W_in"], np.float32)
    wint = _ct(win.reshape(8, 128, 32, 128).transpose(2, 1, 0, 3)
               .reshape(32 * 128, 8 * 128))
    cw = np.asarray(inputs["L1_attn_b_conv_w"], np.float32)
    convw = np.concatenate([blk128(cw[:, 0, k]) for k in range(4)], axis=1)
    shared = dict(wint=wint,
                  binrow=_ct(np.asarray(inputs["L1_attn_b_b_in"],
                                        np.float32)[None, :]),
                  convw=_ct(convw),
                  cbblk=blk128(np.asarray(inputs["L1_attn_b_conv_b"],
                                          np.float32)),
                  xpw=_ct(np.asarray(inputs["L1_attn_b_x_proj_w"],
                                     np.float32)),
                  dtw=_ct(np.asarray(inputs["L1_attn_b_dt_proj_w"],
                                     np.float32)),
                  dtbrow=_ct(np.asarray(inputs["L1_attn_b_dt_bias"],
                                        np.float32)[None, :]),
                  fconsts=host_fconsts(FCONST_VALUES),
                  **{k: consts[k] for k in ["ones_col_bf", "ones_row_f",
                                            "ones_row512"]})
    zeros = np.zeros((D, 128), np.float32)
    bm0 = np.ones((1, 2176), np.float32)
    bm0[0, :128] = 0.0
    bm1 = np.ones((1, 2176), np.float32)
    ims = []
    for c in range(NCORES):
        s = c % 2
        xt = zeros if s == 0 else _ct(X[(c // 2) * 2][:, TLOC - 128:])
        ims.append(dict(xqT=X[c], xtT=xt, bmask=(bm0 if s == 0 else bm1),
                        **shared))
    return ims


def run_mamba_pre(X, inputs):
    if "mpre" not in _CACHE:
        _CACHE["mpre"] = build_mamba_pre()
    im = host_mamba_pre_inmaps(X, inputs)
    res = run_bass_kernel_spmd(_CACHE["mpre"], im, core_ids=list(range(NCORES)))
    return res.results


def build_mamba_scan():
    nc = bass.Bass("TRN2", target_bir_lowering=False, debug=False)
    TT = L
    EH = 1024

    xch = nc.dram_tensor("xch", [EH, TT], BF16, kind="ExternalInput").ap()
    deltah = nc.dram_tensor("deltah", [EH, TT], BF16,
                            kind="ExternalInput").ap()
    xrh = nc.dram_tensor("xrh", [EH, TT], BF16, kind="ExternalInput").ap()
    brow = nc.dram_tensor("brows", [16, TT], BF16, kind="ExternalInput").ap()
    crow = nc.dram_tensor("crows", [16, TT], BF16, kind="ExternalInput").ap()
    alogt = nc.dram_tensor("alogt", [128, 128], F32,
                           kind="ExternalInput").ap()
    dparb = nc.dram_tensor("dparb", [128, 8], F32, kind="ExternalInput").ap()
    sel16 = nc.dram_tensor("sel16", [128, 16 * 128], BF16,
                           kind="ExternalInput").ap()
    e16c = nc.dram_tensor("e16c", [8, 128], BF16, kind="ExternalInput").ap()
    rep8 = nc.dram_tensor("rep8", [16, 128], BF16, kind="ExternalInput").ap()
    yxr = nc.dram_tensor("yxr", [EH, TT], BF16, kind="ExternalOutput").ap()

    es = ExitStack()
    p = Prog(nc)
    ctx = Ctx(nc, es, p)
    load_fconsts(ctx, FCONST_VALUES)

    brow_sb, brow_sb_b = ctx.sbuf("brow_sb", [16, TT], BF16)
    crow_sb, crow_sb_b = ctx.sbuf("crow_sb", [16, TT], BF16)
    brep, brep_b = ctx.sbuf("brep", [128, TT], BF16)
    crep, crep_b = ctx.sbuf("crep", [128, TT], BF16)
    a_sb, a_sb_b = ctx.sbuf("a_sb", [128, 128], F32)
    dp_sb, dp_sb_b = ctx.sbuf("dp_sb", [128, 8], F32)
    sel_sb, sel_sb_b = ctx.sbuf("sel_sb", [128, 16 * 128], BF16)
    e16_sb, e16_sb_b = ctx.sbuf("e16_sb", [8, 128], BF16)
    rep_sb, rep_sb_b = ctx.sbuf("rep_sb", [16, 128], BF16)
    delta_c, _d = ctx.sbuf("delta_c", [128, TT], BF16)
    delta_c_b = Buf("delta_c")
    xc_c, _x = ctx.sbuf("xc_c", [128, TT], BF16)
    xc_c_b = Buf("xc_c")
    xr_c, _r = ctx.sbuf("xr_c", [128, TT], BF16)
    xr_c_b = Buf("xr_c")
    du_c, du_c_b = ctx.sbuf("du_c", [128, TT], BF16)
    d0, d0_b = ctx.sbuf("d0", [8, TT], BF16)
    u0, u0_b = ctx.sbuf("u0", [8, TT], BF16)
    dA_g, dA_g_b = ctx.sbuf("dA_g", [128, TT], BF16)
    w_g, w_g_b = ctx.sbuf("w_g", [128, TT], BF16)
    h_g, h_g_b = ctx.sbuf("h_g", [128, TT], F32)
    hc_g, hc_g_b = ctx.sbuf("hc_g", [128, TT], BF16)
    yt1, yt1_b = ctx.sbuf("yt1", [128, 512], F32)
    yout, _yo = ctx.sbuf("yout", [128, 2 * 512], BF16)
    yout_bufs = [Buf("yo0"), Buf("yo1")]

    psY, psY_b = ctx.psum("psY", [128, 512], F32)
    psE1, psE1_b = ctx.psum("psE1", [128, 512], F32)
    psE2, psE2_b = ctx.psum("psE2", [128, 512], F32)
    psB, psB_b = ctx.psum("psB", [128, 512], F32)

    p.dma("sync", out=brow_sb[:], in_=brow, writes=[brow_sb_b], tag="lb")
    p.dma("sync", out=crow_sb[:], in_=crow, writes=[crow_sb_b], tag="lc")
    p.dma("sync", out=a_sb[:], in_=alogt, writes=[a_sb_b], tag="la")
    p.dma("sync", out=dp_sb[:], in_=dparb, writes=[dp_sb_b], tag="ldp")
    p.dma("sync", out=sel_sb[:], in_=sel16, writes=[sel_sb_b], tag="lsel")
    p.dma("sync", out=e16_sb[:], in_=e16c, writes=[e16_sb_b], tag="le16")
    p.dma("sync", out=rep_sb[:], in_=rep8, writes=[rep_sb_b], tag="lrep")
    p.op("scalar", lambda s: s.activation(out=a_sb[:], in_=a_sb[:],
                                          func=AF.Exp),
         reads=[a_sb_b], writes=[a_sb_b], tag="aexp")
    p.op("vector", lambda v: v.tensor_scalar_mul(a_sb[:], a_sb[:], -1.0),
         reads=[a_sb_b], writes=[a_sb_b], tag="aneg")
    for (src_sb, src_b, dst, dst_b, tg) in [(brow_sb, brow_sb_b, brep, brep_b,
                                             "br"),
                                            (crow_sb, crow_sb_b, crep, crep_b,
                                             "cr")]:
        for j0 in range(0, TT, 512):
            def em_r(te, src_sb=src_sb, j0=j0):
                return te.matmul(psB[:], rep_sb[:],
                                 src_sb[:, j0:j0 + 512], start=True,
                                 stop=True)
            p.op("tensor", em_r, reads=[src_b, rep_sb_b], writes=[psB_b],
                 tag=f"{tg}m{j0}")
            p.op("vector", lambda v, dst=dst, j0=j0: v.tensor_copy(
                dst[:, j0:j0 + 512], psB[:]), reads=[psB_b], writes=[dst_b],
                tag=f"{tg}e{j0}")

    dus = nc.dram_tensor("dus", [EH, TT], BF16)
    dus_b = Buf("dus")
    y_sb, y_sb_b = ctx.sbuf("y_sb", [128, TT], F32)
    tok = None
    for ch in range(8):
        p.dma("sync", out=delta_c[:], in_=deltah[ch * 128:(ch + 1) * 128, :],
              writes=[delta_c_b], tag=f"ld{ch}")
        p.dma("sync", out=xc_c[:], in_=xch[ch * 128:(ch + 1) * 128, :],
              writes=[xc_c_b], tag=f"lx{ch}")
        p.dma("sync", out=xr_c[:], in_=xrh[ch * 128:(ch + 1) * 128, :],
              writes=[xr_c_b], tag=f"lr{ch}")
        p.op("vector", lambda v: v.tensor_mul(du_c[:], delta_c[:], xc_c[:]),
             reads=[delta_c_b, xc_c_b], writes=[du_c_b], tag=f"du{ch}")
        p.dma("sync", out=dus[ch * 128:(ch + 1) * 128, :], in_=du_c[:],
              reads=[du_c_b], writes=[dus_b], tag=f"duw{ch}")
        for g in range(16):
            gl = ch * 16 + g
            p.dma("gpsimd", out=d0[:],
                  in_=deltah[ch * 128 + g * 8: ch * 128 + (g + 1) * 8, :],
                  writes=[d0_b], tag=f"bd{ch}_{g}")
            p.dma("gpsimd", out=u0[:],
                  in_=dus[ch * 128 + g * 8: ch * 128 + (g + 1) * 8, :],
                  reads=[dus_b], writes=[u0_b], tag=f"bu{ch}_{g}")
            for r in range(8):
                r0 = r * 512

                def em_e1(te, r0=r0):
                    return te.matmul(psE1[:], e16_sb[:], d0[:, r0:r0 + 512],
                                     start=True, stop=True)
                p.op("tensor", em_e1, reads=[d0_b, e16_sb_b], writes=[psE1_b],
                     tag=f"e1{ch}_{g}_{r}")

                def em_da(s, gl=gl, r0=r0):
                    return s.activation(out=dA_g[:, r0:r0 + 512],
                                        in_=psE1[:], func=AF.Exp,
                                        scale=a_sb[:, gl:gl + 1])
                p.op("scalar", em_da, reads=[psE1_b, a_sb_b], writes=[dA_g_b],
                     tag=f"da{ch}_{g}_{r}")

                def em_e2(te, r0=r0):
                    return te.matmul(psE2[:], e16_sb[:], u0[:, r0:r0 + 512],
                                     start=True, stop=True)
                p.op("tensor", em_e2, reads=[u0_b, e16_sb_b], writes=[psE2_b],
                     tag=f"e2{ch}_{g}_{r}")

                def em_w(v, r0=r0):
                    return v.tensor_mul(w_g[:, r0:r0 + 512], psE2[:],
                                        brep[:, r0:r0 + 512])
                p.op("vector", em_w, reads=[psE2_b, brep_b], writes=[w_g_b],
                     tag=f"w{ch}_{g}_{r}")

            def em_scan(v):
                return v.tensor_tensor_scan(h_g[:], dA_g[:], w_g[:], 0.0,
                                            ALU.mult, ALU.add)
            p.op("vector", em_scan, reads=[dA_g_b, w_g_b], writes=[h_g_b],
                 tag=f"sc{ch}_{g}")

            def em_hc(v):
                return v.tensor_mul(hc_g[:], h_g[:], crep[:])
            p.op("vector", em_hc, reads=[h_g_b, crep_b], writes=[hc_g_b],
                 tag=f"hc{ch}_{g}")
            for r in range(8):
                r0 = r * 512

                def em_y(te, g=g, r0=r0):
                    return te.matmul(psY[:], sel_sb[:, g * 128:(g + 1) * 128],
                                     hc_g[:, r0:r0 + 512], start=True,
                                     stop=True)
                p.op("tensor", em_y, reads=[hc_g_b, sel_sb_b], writes=[psY_b],
                     tag=f"y{ch}_{g}_{r}")
                if g == 0:
                    p.op("vector", lambda v, r0=r0: v.tensor_copy(
                        y_sb[:, r0:r0 + 512], psY[:]), reads=[psY_b],
                        writes=[y_sb_b], tag=f"yc{ch}_{g}_{r}")
                else:
                    def em_ya(v, r0=r0):
                        return v.tensor_add(y_sb[:, r0:r0 + 512],
                                            y_sb[:, r0:r0 + 512], psY[:])
                    p.op("vector", em_ya, reads=[psY_b, y_sb_b],
                         writes=[y_sb_b], tag=f"ya{ch}_{g}_{r}")
        for r in range(8):
            r0 = r * 512
            ybi = r % 2

            def em_t1(v, ch=ch, r0=r0):
                return v.scalar_tensor_tensor(yt1[:], xc_c[:, r0:r0 + 512],
                                              dp_sb[:, ch:ch + 1],
                                              y_sb[:, r0:r0 + 512],
                                              ALU.mult, ALU.add)
            p.op("vector", em_t1, reads=[xc_c_b, dp_sb_b, y_sb_b],
                 writes=[yt1_b], tag=f"t1{ch}_{r}")

            def em_t2(v, r0=r0, ybi=ybi):
                return v.tensor_mul(yout[:, ybi * 512:(ybi + 1) * 512],
                                    yt1[:], xr_c[:, r0:r0 + 512])
            p.op("vector", em_t2, reads=[yt1_b, xr_c_b],
                 writes=[yout_bufs[ybi]], tag=f"t2{ch}_{r}")
            tok = p.dma("sync", out=yxr[ch * 128:(ch + 1) * 128, r0:r0 + 512],
                        in_=yout[:, ybi * 512:(ybi + 1) * 512],
                        reads=[yout_bufs[ybi]], tag=f"yo{ch}_{r}")
    p.finish_outputs([tok])
    p.emit(es)
    es.close()
    return nc


def host_mamba_scan_inmaps(pre_results, inputs):
    consts = host_consts()
    alog = np.asarray(inputs["L1_attn_b_A_log"], np.float32)
    dpar = np.asarray(inputs["L1_attn_b_D_param"], np.float32)
    sel = np.zeros((128, 16, 128), BF)
    for g in range(16):
        for lane in range(128):
            sel[lane, g, g * 8 + lane // 16] = 1
    e16 = np.zeros((8, 128), BF)
    for k in range(8):
        e16[k, k * 16:(k + 1) * 16] = 1
    rep = np.zeros((16, 128), BF)
    for k in range(16):
        rep[k, k::16] = 1
    shared = dict(sel16=_ct(sel.reshape(128, 16 * 128)), e16c=e16, rep8=rep,
                  fconsts=host_fconsts(FCONST_VALUES))
    ims = []
    for c in range(NCORES):
        b, s = c // 2, c % 2
        e0, e1 = pre_results[2 * b], pre_results[2 * b + 1]
        xc = np.concatenate([e0["xcs"], e1["xcs"]], axis=1)
        dt = np.concatenate([e0["dts"], e1["dts"]], axis=1)
        xr = np.concatenate([e0["xrs"], e1["xrs"]], axis=1)
        bcv = np.concatenate([e0["bcs"], e1["bcs"]], axis=1)
        half = slice(s * 1024, (s + 1) * 1024)
        alog_h = alog[s * 1024:(s + 1) * 1024]
        alogt = np.zeros((128, 128), np.float32)
        for g in range(128):
            for lane in range(128):
                alogt[lane, g] = alog_h[g * 8 + lane // 16, lane % 16]
        dph = blk128(dpar[s * 1024:(s + 1) * 1024])
        ims.append(dict(xch=_ct(xc[half]), deltah=_ct(dt[half]),
                        xrh=_ct(xr[half]), brows=_ct(bcv[:16]),
                        crows=_ct(bcv[16:32]), alogt=_ct(alogt), dparb=dph,
                        **shared))
    return ims


def run_mamba_scan(pre_results, inputs):
    if "mscan" not in _CACHE:
        _CACHE["mscan"] = build_mamba_scan()
    im = host_mamba_scan_inmaps(pre_results, inputs)
    res = run_bass_kernel_spmd(_CACHE["mscan"], im,
                               core_ids=list(range(NCORES)))
    return [res.results[c]["yxr"] for c in range(NCORES)]


def build_mamba_out():
    nc = bass.Bass("TRN2", target_bir_lowering=False, debug=False)
    T = TLOC
    E = 2048
    ECH = 16

    yxrT = nc.dram_tensor("yxrT", [E, T], BF16, kind="ExternalInput").ap()
    xT = nc.dram_tensor("xT", [D, T], F32, kind="ExternalInput").ap()
    woutm = nc.dram_tensor("woutm", [E, D], F32, kind="ExternalInput").ap()
    borow = nc.dram_tensor("borow", [1, D], F32, kind="ExternalInput").ap()
    yT = nc.dram_tensor("yT", [D, T], F32, kind="ExternalOutput").ap()

    es = ExitStack()
    p = Prog(nc)
    ctx = Ctx(nc, es, p)
    consts = load_consts(ctx, ["ones_row512"])
    ones512, ones512_bb = consts["ones_row512"]

    borow_sb, borow_sb_b = ctx.sbuf("borow_sb", [1, D], BF16)
    yx_all, _yx = ctx.sbuf("yx_all", [128, ECH * T], BF16)
    yxbufs = [Buf(f"yx{c}") for c in range(ECH)]
    wo_all, _wo = ctx.sbuf("wo_all", [128, ECH * D], BF16)
    wo_bufs = [Buf(f"wo{c}") for c in range(ECH)]
    xres, xres_b = ctx.sbuf("xres", [128, 512], F32)
    ysb, _ys = ctx.sbuf("ysb", [128, 2 * 512], F32)
    ysb_bufs = [Buf("ysb0"), Buf("ysb1")]
    psW = [ctx.psum(f"psW{i}", [128, 512], F32) for i in range(2)]

    p.dma("gpsimd", out=borow_sb[:], in_=borow, writes=[borow_sb_b], tag="lb")
    for c in range(ECH):
        p.dma("sync", out=yx_all[:, c * T:(c + 1) * T],
              in_=yxrT[c * 128:(c + 1) * 128, :], writes=[yxbufs[c]],
              tag=f"lyx{c}")
        p.dma("gpsimd", out=wo_all[:, c * D:(c + 1) * D],
              in_=woutm[c * 128:(c + 1) * 128, :], writes=[wo_bufs[c]],
              tag=f"lwo{c}")
    tok = None
    for j2 in range(0, T, 512):
        for dc in range(DCH):
            pi = psW[dc % 2]

            def em_w(te, dc=dc, j2=j2, pi=pi):
                for c in range(ECH):
                    te.matmul(pi[0][:],
                              wo_all[:, c * D + dc * 128:
                                     c * D + dc * 128 + 128],
                              yx_all[:, c * T + j2: c * T + j2 + 512],
                              start=(c == 0), stop=False)
                return te.matmul(pi[0][:],
                                 borow_sb[0:1, dc * 128:(dc + 1) * 128],
                                 ones512[:], start=False, stop=True)
            p.op("tensor", em_w, reads=yxbufs + wo_bufs + [borow_sb_b,
                                                           ones512_bb],
                 writes=[pi[1]], tag=f"w{j2}_{dc}")
            p.dma("sync", out=xres[:],
                  in_=xT[dc * 128:(dc + 1) * 128, j2:j2 + 512],
                  writes=[xres_b], tag=f"wx{j2}_{dc}")
            yi = dc % 2

            def em_y(v, pi=pi, yi=yi):
                return v.tensor_add(ysb[:, yi * 512:(yi + 1) * 512],
                                    pi[0][:], xres[:])
            p.op("vector", em_y, reads=[pi[1], xres_b], writes=[ysb_bufs[yi]],
                 tag=f"wy{j2}_{dc}")
            tok = p.dma("sync", out=yT[dc * 128:(dc + 1) * 128, j2:j2 + 512],
                        in_=ysb[:, yi * 512:(yi + 1) * 512],
                        reads=[ysb_bufs[yi]], tag=f"wst{j2}_{dc}")
    p.finish_outputs([tok])
    p.emit(es)
    es.close()
    return nc


def run_mamba_out(YX, X, inputs):
    if "mout" not in _CACHE:
        _CACHE["mout"] = build_mamba_out()
    consts = host_consts()
    shared = dict(woutm=_ct(np.asarray(inputs["L1_attn_b_W_out"], np.float32)),
                  borow=_ct(np.asarray(inputs["L1_attn_b_b_out"],
                                       np.float32)[None, :]),
                  ones_row512=consts["ones_row512"])
    ims = []
    for c in range(NCORES):
        b, s = c // 2, c % 2
        full = np.concatenate([YX[2 * b], YX[2 * b + 1]], axis=0)
        ims.append(dict(yxrT=_ct(full[:, s * TLOC:(s + 1) * TLOC]),
                        xT=X[c], **shared))
    res = run_bass_kernel_spmd(_CACHE["mout"], ims, core_ids=list(range(NCORES)))
    return [res.results[c]["yT"] for c in range(NCORES)]


def x_to_shards(x):
    out = []
    for c in range(NCORES):
        b, s = c // 2, c % 2
        out.append(_ct(np.asarray(x[b, s * TLOC:(s + 1) * TLOC, :]).T
                       .astype(np.float32, copy=False)))
    return out


def shards_to_x(X):
    out = np.empty((B, L, D), np.float32)
    for c in range(NCORES):
        b, s = c // 2, c % 2
        out[b, s * TLOC:(s + 1) * TLOC, :] = X[c].T
    return out


_CACHE = {}


def run_bitnet(X, inputs, lname):
    if "bitnet" not in _CACHE:
        _CACHE["bitnet"] = build_bitnet()
    im = host_bitnet_inmaps(X, inputs, lname)
    res = run_bass_kernel_spmd(_CACHE["bitnet"], im, core_ids=list(range(NCORES)))
    return [res.results[c]["yT"] for c in range(NCORES)]


def kernel(**inputs):
    x = np.asarray(inputs["L0_attn_a_hidden_states"], np.float32)
    X = x_to_shards(x)
    X = run_retnet(X, inputs)
    X = run_bitnet(X, inputs, "L0_ffn")
    X = run_attnb(X, inputs, "L0_attn_b")
    X = run_bitnet(X, inputs, "L0_ffn")
    X = run_mla(X, inputs)
    wbf = run_wcast(inputs)
    X = run_moe(X, inputs, wbf)
    pre = run_mamba_pre(X, inputs)
    yxr = run_mamba_scan(pre, inputs)
    X = run_mamba_out(yxr, X, inputs)
    X = run_moe(X, inputs, wbf)
    return shards_to_x(X)
